# version 12
# baseline (speedup 1.0000x reference)
import numpy as np
import ml_dtypes
from contextlib import ExitStack

import concourse.bass as bass
import concourse.mybir as mybir
from concourse.bass_utils import run_bass_kernel_spmd

F32 = mybir.dt.float32
BF16 = mybir.dt.bfloat16
U32 = mybir.dt.uint32
AF = mybir.ActivationFunctionType
ALU = mybir.AluOpType
AXL = mybir.AxisListType

B, T, C = 2, 4096, 2048
NH, HD, HH, H = 8, 256, 128, 8192
TS = 1024
LAM_INIT = 0.2
RMS_EPS = 1.1920929e-07
LN_EPS = 1e-5
FW_SCALE = float(np.float32(1.0) / np.float32(np.sqrt(np.float32(8192.0))))
ROPE_BASE = 10000.0

STOP_AFTER = 99


class Sem:
    def __init__(self, stack, nc, name):
        self.h = stack.enter_context(nc.semaphore(name))
        self.v = 0

    def inc(self, inst, n=1):
        inst.then_inc(self.h, n)
        self.v += n
        return inst

    def dinc(self, inst):
        inst.then_inc(self.h, 16)
        self.v += 16
        return inst


def host_consts():
    p = np.arange(128)[:, None, None]
    o = np.arange(4)[None, :, None]
    f = np.arange(512)[None, None, :]
    masks = (f >= p + 128 * o).astype(ml_dtypes.bfloat16)
    i = np.arange(128)
    pc = np.array([bin(v).count("1") for v in range(128)])
    h128 = ((-1.0) ** pc[(i[:, None] & i[None, :])]).astype(ml_dtypes.bfloat16)
    j64 = np.arange(0, HH, 2, dtype=np.float64) / HH
    inv_freq = (1.0 / ROPE_BASE) ** j64
    ropePT = np.zeros((NH, 128, 128), dtype=np.float64)
    for h in range(NH):
        cos = np.cos(h * inv_freq)
        sin = np.sin(h * inv_freq)
        P = np.zeros((128, 128))
        for d in range(64):
            P[d, d] = cos[d]
            P[d, d + 64] = sin[d]
            P[d + 64, d] = -sin[d]
            P[d + 64, d + 64] = cos[d]
        ropePT[h] = P.T
    return (masks, h128, ropePT.astype(np.float32),
            np.eye(128, dtype=ml_dtypes.bfloat16), np.eye(128, dtype=np.float32))


def build_program():
    nc = bass.Bass("TRN2", target_bir_lowering=False)

    x_full = nc.dram_tensor("x_full", [T, C], F32, kind="ExternalInput").ap()
    x_mine = nc.dram_tensor("x_mine", [TS, C], F32, kind="ExternalInput").ap()
    wq_s = nc.dram_tensor("wq_s", [512, C], F32, kind="ExternalInput").ap()
    wk_s = nc.dram_tensor("wk_s", [512, C], F32, kind="ExternalInput").ap()
    wv_s = nc.dram_tensor("wv_s", [512, C], F32, kind="ExternalInput").ap()
    wo_in = nc.dram_tensor("wo_in", [C, C], F32, kind="ExternalInput").ap()
    w1_in = nc.dram_tensor("w1_in", [H, C], F32, kind="ExternalInput").ap()
    w3_in = nc.dram_tensor("w3_in", [H, C], F32, kind="ExternalInput").ap()
    w2_in = nc.dram_tensor("w2_in", [C, H], F32, kind="ExternalInput").ap()
    lq1_in = nc.dram_tensor("lq1_in", [128, 2], F32, kind="ExternalInput").ap()
    lk1_in = nc.dram_tensor("lk1_in", [128, 2], F32, kind="ExternalInput").ap()
    lq2_in = nc.dram_tensor("lq2_in", [128, 2], F32, kind="ExternalInput").ap()
    lk2_in = nc.dram_tensor("lk2_in", [128, 2], F32, kind="ExternalInput").ap()
    masks_in = nc.dram_tensor("masks_in", [128, 4, 512], BF16, kind="ExternalInput").ap()
    h128_in = nc.dram_tensor("h128_in", [128, 128], BF16, kind="ExternalInput").ap()
    ropept_in = nc.dram_tensor("ropept_in", [128, 2, 128], F32, kind="ExternalInput").ap()
    identb_in = nc.dram_tensor("identb_in", [128, 128], BF16, kind="ExternalInput").ap()
    identf_in = nc.dram_tensor("identf_in", [128, 128], F32, kind="ExternalInput").ap()
    jidx_in = nc.dram_tensor("jidx_in", [128, 32], U32, kind="ExternalInput").ap()
    out = nc.dram_tensor("out", [TS, C], F32, kind="ExternalOutput").ap()

    xn_scr = nc.dram_tensor("xn_scr", [T, C], BF16)
    xnt_scr = nc.dram_tensor("xnt_scr", [C, T], BF16)
    xnm_scr = nc.dram_tensor("xnm_scr", [TS, C], BF16)
    wv_scr = nc.dram_tensor("wv_scr", [512, C], BF16)
    wvt_scr = nc.dram_tensor("wvt_scr", [C, 512], BF16)
    wo_scr = nc.dram_tensor("wo_scr", [C, C], BF16)
    w1_scr = nc.dram_tensor("w1_scr", [H, C], BF16)
    w1t_scr = nc.dram_tensor("w1t_scr", [C, H], BF16)
    w3_scr = nc.dram_tensor("w3_scr", [H, C], BF16)
    w3t_scr = nc.dram_tensor("w3t_scr", [C, H], BF16)
    w2_scr = nc.dram_tensor("w2_scr", [C, H], BF16)
    w2t_scr = nc.dram_tensor("w2t_scr", [H, C], BF16)
    ycc_in = nc.dram_tensor("ycc_in", [T, 512], BF16)
    ycc_out = nc.dram_tensor("ycc_out", [4 * T, 512], BF16)
    attn_scr = nc.dram_tensor("attn_scr", [C, TS], F32)
    ysl_scr = nc.dram_tensor("ysl_scr", [T, 512], BF16)

    st_all = ExitStack()
    with st_all:
        def S(name):
            return Sem(st_all, nc, name)

        d_c = S("d_c"); dqk = [S(f"dqk{i}") for i in range(2)]
        dx = [S(f"dx{i}") for i in range(3)]
        dxn = [S(f"dxn{i}") for i in range(2)]
        dwci = [S(f"dwci{i}") for i in range(2)]
        dwco = [S(f"dwco{i}") for i in range(2)]
        dtr = [S(f"dtr{i}") for i in range(2)]
        dtrx = S("dtrx"); dtwx = S("dtwx")
        dtw = [S(f"dtw{i}") for i in range(2)]
        d_xnt = S("d_xnt"); d_wvt = S("d_wvt"); d_wot = S("d_wot")
        dw13 = [S(f"dw13_{i}") for i in range(2)]
        dw2t = [S(f"dw2t{i}") for i in range(2)]
        dy = [S(f"dy{i}") for i in range(2)]
        dycc = [S(f"dycc{i}") for i in range(2)]
        dysw = [S(f"dysw{i}") for i in range(2)]
        d_yt2 = S("d_yt2")
        datt = [S(f"datt{i}") for i in range(2)]
        datr = [S(f"datr{i}") for i in range(2)]
        d_xr = S("d_xr"); d_xnmt = S("d_xnmt"); d_out = S("d_out"); d_msk = S("d_msk")
        c_cc = S("c_cc"); c_cast = S("c_cast"); c_lam = S("c_lam")
        c_var = S("c_var"); c_sx2 = S("c_sx2"); c_lns = S("c_lns"); c_lnx = S("c_lnx")
        c_fmm = S("c_fmm"); c_fold = S("c_fold")
        c_pmm = S("c_pmm"); c_sq = S("c_sq")
        c_msq = S("c_msq"); c_rstd = S("c_rstd"); c_bcast = S("c_bcast"); c_qhat = S("c_qhat")
        c_vfin = S("c_vfin"); c_vev = S("c_vev")
        c_sc = S("c_sc"); c_exp = S("c_exp"); c_msk = S("c_msk")
        c_pvt = S("c_pvt"); c_pv = S("c_pv"); c_rec = S("c_rec"); c_oev = S("c_oev")
        c_wo = S("c_wo"); c_woev = S("c_woev")
        c_gl = S("c_gl"); c_sil = S("c_sil"); c_hid = S("c_hid")
        c_h128 = S("c_h128"); c_hev = S("c_hev"); c_bf = S("c_bf")
        c_w2 = S("c_w2"); c_w2h = S("c_w2h"); c_w2e = S("c_w2e"); c_tr = S("c_tr"); c_fin = S("c_fin"); c_mev = S("c_mev")
        bV = S("bV"); bS = S("bS")

        E = st_all.enter_context
        h128_sb = E(nc.sbuf_tensor("h128_sb", [128, 128], BF16))
        ropept_sb = E(nc.sbuf_tensor("ropept_sb", [128, 2, 128], F32))
        identb_sb = E(nc.sbuf_tensor("identb_sb", [128, 128], BF16))
        identf_sb = E(nc.sbuf_tensor("identf_sb", [128, 128], F32))
        lam_sb = E(nc.sbuf_tensor("lam_sb", [128, 8], F32))
        jidx_sb = E(nc.sbuf_tensor("jidx_sb", [128, 32], U32))
        ones_col_bf = E(nc.sbuf_tensor("ones_col_bf", [128, 1], BF16))
        ones_row_f = E(nc.sbuf_tensor("ones_row_f", [1, 128], F32))
        eps_ln = E(nc.sbuf_tensor("eps_ln", [128, 1], F32))
        eps_rms = E(nc.sbuf_tensor("eps_rms", [1, 1], F32))
        qbias = E(nc.sbuf_tensor("qbias", [1, 1], F32))
        zbias = E(nc.sbuf_tensor("zbias", [1, 1], F32))
        zcol = E(nc.sbuf_tensor("zcol", [128, 1], F32))
        lamdots = E(nc.sbuf_tensor("lamdots", [1, 2], F32))
        lamexp = E(nc.sbuf_tensor("lamexp", [128, 2], F32))
        neglam = E(nc.sbuf_tensor("neglam", [128, 1], F32))
        ps = [E(nc.psum_tensor(f"ps{i}", [128, 512], F32)) for i in range(7)]

        STOP = STOP_AFTER

        def tkc(s):
            return 4 * (s + 1)

        exp_idx, pvt_idx, grp_idx, msk_idx = {}, {}, {}, {}
        EXPCUM = {0: 0}
        n = m = g = mk = 0
        for hh in range(2):
            for s in range(8):
                for i in range(2):
                    g += 1
                    grp_idx[(hh, i, s)] = g
                    for t in range(tkc(s)):
                        n += 1
                        exp_idx[(hh, i, s, t)] = n
                        if t < tkc(s) - 1:
                            m += 1
                            pvt_idx[(hh, i, s, t)] = m
                    for t in range(4 * s, 4 * s + 4):
                        mk += 1
                        msk_idx[(hh, i, s, t)] = mk
                    EXPCUM[g] = n

        cast_tiles = []
        for (srcT, dstT, nr, ncol, scale) in (
                (wv_s, wv_scr, 512, C, None),
                (wo_in, wo_scr, C, C, 1.0 - LAM_INIT),
                (w1_in, w1_scr, H, C, None),
                (w3_in, w3_scr, H, C, None),
                (w2_in, w2_scr, C, H, None)):
            for r in range(nr // 128):
                for ch in range(ncol // 1024):
                    cast_tiles.append((srcT[128 * r:128 * (r + 1), 1024 * ch:1024 * (ch + 1)],
                                       dstT[128 * r:128 * (r + 1), 1024 * ch:1024 * (ch + 1)],
                                       scale))
        N_CAST = len(cast_tiles)
        assert N_CAST == 424
        CAST_WV, CAST_WO, CAST_W1, CAST_W3, CAST_W2 = 8, 40, 168, 296, 424
        CAST_B1, CAST_B3 = 40, 296

        def cast_dma_steps(sync, t_lo, t_hi, cwf, cwb):
            for t in range(t_lo, t_hi):
                if t == 0:
                    dwci[0].dinc(sync.dma_start(out=cwf[:, 0, :], in_=cast_tiles[0][0]))
                    dwci[1].dinc(sync.dma_start(out=cwf[:, 1, :], in_=cast_tiles[1][0]))
                sync.wait_ge(c_cast.h, t + 1)
                dwco[t % 2].dinc(sync.dma_start(out=cast_tiles[t][1], in_=cwb[:, t % 2, :]))
                if t + 2 < N_CAST:
                    dwci[t % 2].dinc(sync.dma_start(out=cwf[:, t % 2, :], in_=cast_tiles[t + 2][0]))

        def cast_ops(vector, t_lo, t_hi, cwf, cwb):
            for t in range(t_lo, t_hi):
                vector.wait_ge(dwci[t % 2].h, 16 * (t // 2 + 1))
                if t >= 2:
                    vector.wait_ge(dwco[t % 2].h, 16 * (t // 2))
                src, dst, scale = cast_tiles[t]
                if scale is None:
                    c_cast.inc(vector.tensor_copy(cwb[:, t % 2, :], cwf[:, t % 2, :]))
                else:
                    c_cast.inc(vector.tensor_scalar_mul(cwb[:, t % 2, :], cwf[:, t % 2, :], scale))

        def wco_wait(eng, n_tiles):
            eng.wait_ge(dwco[0].h, 16 * ((n_tiles + 1) // 2))
            eng.wait_ge(dwco[1].h, 16 * (n_tiles // 2))

        strips = []
        for fc in range(16):
            strips.append((wv_scr[0:512, 128 * fc:128 * (fc + 1)],
                           wvt_scr[128 * fc:128 * (fc + 1), 0:512], 512))
        for fc in range(16):
            for hf in range(2):
                strips.append((w1_scr[4096 * hf:4096 * (hf + 1), 128 * fc:128 * (fc + 1)],
                               w1t_scr[128 * fc:128 * (fc + 1), 4096 * hf:4096 * (hf + 1)], 4096))
        for fc in range(16):
            for hf in range(2):
                strips.append((w3_scr[4096 * hf:4096 * (hf + 1), 128 * fc:128 * (fc + 1)],
                               w3t_scr[128 * fc:128 * (fc + 1), 4096 * hf:4096 * (hf + 1)], 4096))
        for hc in range(64):
            strips.append((w2_scr[0:C, 128 * hc:128 * (hc + 1)],
                           w2t_scr[128 * hc:128 * (hc + 1), 0:C], C))
        N_STRIP = len(strips)
        XN_S, WV_S, W1_S, W3_S, W2_S = 0, 16, 48, 80, 144
        assert N_STRIP == 144

        def strip_steps(sync, i_lo, i_hi, tb):
            for i in range(i_lo, i_hi):
                src, dst, width = strips[i]
                if i >= 2:
                    sync.wait_ge(dtw[i % 2].h, 16 * (i // 2))
                dtr[i % 2].dinc(sync.dma_start_transpose(tb[:, i % 2, 0:width], src))
                sync.wait_ge(dtr[i % 2].h, 16 * (i // 2 + 1))
                dtw[i % 2].dinc(sync.dma_start(out=dst, in_=tb[:, i % 2, 0:width]))

        def xn_strip_steps(sync, sl, tbuf):
            def slot(fc):
                return tbuf[:, fc % 2, 512 * (fc // 2):512 * (fc // 2 + 1)]
            if sl >= 1:
                sync.wait_ge(dtwx.h, 256 * sl)
            for fc in range(16):
                dtrx.dinc(sync.dma_start_transpose(
                    slot(fc), xn_scr[512 * sl:512 * (sl + 1), 128 * fc:128 * (fc + 1)]))
            sync.wait_ge(dtrx.h, 256 * (sl + 1))
            for fc in range(16):
                dtwx.dinc(sync.dma_start(
                    out=xnt_scr[128 * fc:128 * (fc + 1), 512 * sl:512 * (sl + 1)],
                    in_=slot(fc)))

        def tw_wait(eng, n_strips):
            eng.wait_ge(dtw[0].h, 16 * ((n_strips + 1) // 2))
            eng.wait_ge(dtw[1].h, 16 * (n_strips // 2))

        with nc.Block() as blk:
            @blk.sync
            def _(sync):
                d_c.dinc(sync.dma_start(out=h128_sb[:, :], in_=h128_in[:, :]))
                d_c.dinc(sync.dma_start(out=ropept_sb[:, :, :], in_=ropept_in[:, :, :]))
                d_c.dinc(sync.dma_start(out=identb_sb[:, :], in_=identb_in[:, :]))
                d_c.dinc(sync.dma_start(out=identf_sb[:, :], in_=identf_in[:, :]))
                d_c.dinc(sync.dma_start(out=jidx_sb[:, :], in_=jidx_in[:, :]))
                for t, ap in enumerate((lq1_in, lk1_in, lq2_in, lk2_in)):
                    d_c.dinc(sync.dma_start(out=lam_sb[:, 2 * t:2 * t + 2], in_=ap[:, :]))

            @blk.vector
            def _(vector):
                vector.memset(ones_col_bf[:, :], 1.0)
                vector.memset(ones_row_f[0:1, :], 1.0)
                vector.memset(eps_ln[:, :], LN_EPS)
                vector.memset(eps_rms[0:1, :], RMS_EPS)
                vector.memset(zbias[0:1, :], 0.0)
                vector.memset(zcol[:, :], 0.0)
                c_lam.inc(vector.memset(qbias[0:1, :], -0.5 * float(np.log(128.0))))

            @blk.tensor
            def _(tensor):
                tensor.wait_ge(d_c.h, d_c.v)
                tensor.wait_ge(c_lam.h, 1)
                tensor.matmul(ps[0][0:1, 0:1], lam_sb[:, 0:1], lam_sb[:, 2:3], start=True, stop=False)
                tensor.matmul(ps[0][0:1, 0:1], lam_sb[:, 1:2], lam_sb[:, 3:4], start=False, stop=True)
                tensor.matmul(ps[1][0:1, 0:1], lam_sb[:, 4:5], lam_sb[:, 6:7], start=True, stop=False)
                c_lam.inc(tensor.matmul(ps[1][0:1, 0:1], lam_sb[:, 5:6], lam_sb[:, 7:8], start=False, stop=True))

            @blk.scalar
            def _(scalar):
                scalar.wait_ge(c_lam.h, 2)
                scalar.copy(lamdots[0:1, 0:1], ps[0][0:1, 0:1])
                c_lam.inc(scalar.copy(lamdots[0:1, 1:2], ps[1][0:1, 0:1]))

        with nc.Block() as blk:
            @blk.tensor
            def _(tensor):
                tensor.wait_ge(c_lam.h, 3)
                c_lam.inc(tensor.matmul(ps[2][:, 0:2], ones_row_f[0:1, :], lamdots[0:1, 0:2],
                                        start=True, stop=True))

            @blk.scalar
            def _(scalar):
                scalar.wait_ge(c_lam.h, 4)
                c_lam.inc(scalar.activation(out=lamexp[:, :], in_=ps[2][:, 0:2], func=AF.Exp))

            @blk.vector
            def _(vector):
                vector.wait_ge(c_lam.h, 5)
                c_lam.inc(vector.tensor_scalar(
                    out=neglam[:, 0:1], in0=lamexp[:, 1:2],
                    scalar1=lamexp[:, 0:1], scalar2=-LAM_INIT,
                    op0=ALU.subtract, op1=ALU.add))
        LAM_DONE = 6

        NT_FULL, NT_MINE = 32, 8
        NT_LN = NT_FULL + NT_MINE
        ln_srcs = [x_full[128 * t:128 * (t + 1), :] for t in range(NT_FULL)] + \
                  [x_mine[128 * t:128 * (t + 1), :] for t in range(NT_MINE)]
        ln_dsts = [xn_scr[128 * t:128 * (t + 1), :] for t in range(NT_FULL)] + \
                  [xnm_scr[128 * t:128 * (t + 1), :] for t in range(NT_MINE)]

        with ExitStack() as stCast:
            EC = stCast.enter_context
            cwf = EC(nc.sbuf_tensor("cwf", [128, 2, 1024], F32))
            cwb = EC(nc.sbuf_tensor("cwb", [128, 2, 1024], BF16))

            with ExitStack() as stMid:
                EM = stMid.enter_context
                qkwT = EM(nc.sbuf_tensor("qkwT", [128, 8, 16, 128], BF16))
                qk_sb = EM(nc.sbuf_tensor("qk_sb", [128, 8, 4096], BF16))
                v_sb = EM(nc.sbuf_tensor("v_sb", [128, 2, 32, 257], BF16))
                tbuf = EM(nc.sbuf_tensor("tbuf", [128, 2, 4096], BF16, is_transpose_dest=True))

                with ExitStack() as stLN:
                    EL = stLN.enter_context
                    xbuf = EL(nc.sbuf_tensor("xbuf", [128, 3, 2048], F32))
                    xnbuf = EL(nc.sbuf_tensor("xnbuf", [128, 2, 2048], BF16))
                    mraw = EL(nc.sbuf_tensor("mraw", [128, 2], F32))
                    negm = EL(nc.sbuf_tensor("negm", [128, 2], F32))
                    sx2 = EL(nc.sbuf_tensor("sx2", [128, 2], F32))
                    m2 = EL(nc.sbuf_tensor("m2", [128, 2], F32))
                    varbuf = EL(nc.sbuf_tensor("varbuf", [128, 2], F32))
                    rstdln = EL(nc.sbuf_tensor("rstdln", [128, 2], F32))
                    qkf = EL(nc.sbuf_tensor("qkf", [128, 2, 2048], F32))

                    qk_tiles = []
                    for hl in range(2):
                        for half in range(2):
                            qk_tiles.append((wq_s[256 * hl + 128 * half:256 * hl + 128 * half + 128, :], hl))
                        for half in range(2):
                            qk_tiles.append((wk_s[256 * hl + 128 * half:256 * hl + 128 * half + 128, :], hl))

                    with nc.Block() as blk:
                        @blk.tensor
                        def _(tensor):
                            if STOP < 1:
                                return
                            tensor.wait_ge(c_lam.h, LAM_DONE)
                            for t in range(8):
                                hl = qk_tiles[t][1]
                                tensor.wait_ge(dqk[t % 2].h, 16 * (t // 2 + 1))
                                for fc in range(16):
                                    k = 16 * t + fc
                                    if k >= 4:
                                        tensor.wait_ge(c_fold.h, k - 3)
                                    c_fmm.inc(tensor.matmul(
                                        ps[k % 4][:, 0:128],
                                        qkf[:, t % 2, 128 * fc:128 * (fc + 1)],
                                        ropept_sb[:, hl, :], start=True, stop=True))

                        @blk.scalar
                        def _(scalar):
                            if STOP < 1:
                                return
                            for t in range(3):
                                dx[t % 3].dinc(scalar.dma_start(out=xbuf[:, t, :], in_=ln_srcs[t]))
                            for t in range(2):
                                dqk[t % 2].dinc(scalar.dma_start(out=qkf[:, t % 2, :],
                                                                 in_=qk_tiles[t][0]))
                            for t in range(NT_LN):
                                w = t % 2
                                scalar.wait_ge(dx[t % 3].h, 16 * (t // 3 + 1))
                                if t >= 2:
                                    scalar.wait_ge(dxn[w].h, 16 * (t // 2))
                                    scalar.wait_ge(c_var.h, t - 1)
                                c_sx2.inc(scalar.activation(
                                    out=xnbuf[:, w, :], in_=xbuf[:, t % 3, :],
                                    func=AF.Square, bias=zcol[:, 0:1],
                                    scale=1.0, accum_out=sx2[:, w:w + 1]))
                                if t >= 1:
                                    scalar.wait_ge(c_lnx.h, t)
                                    dxn[(t - 1) % 2].dinc(scalar.dma_start(
                                        out=ln_dsts[t - 1], in_=xnbuf[:, (t - 1) % 2, :]))
                                    if t + 2 < NT_LN:
                                        dx[(t + 2) % 3].dinc(scalar.dma_start(
                                            out=xbuf[:, (t + 2) % 3, :], in_=ln_srcs[t + 2]))
                                scalar.wait_ge(c_var.h, t + 1)
                                bS.inc(scalar.activation(out=rstdln[:, w:w + 1],
                                                         in_=varbuf[:, w:w + 1],
                                                         func=AF.Ln, bias=eps_ln[:, 0:1],
                                                         scale=1.0))
                                scalar.wait_ge(bS.h, bS.v)
                                c_lns.inc(scalar.activation(out=rstdln[:, w:w + 1],
                                                            in_=rstdln[:, w:w + 1],
                                                            func=AF.Exp, scale=-0.5))
                                if 8 <= t < 40:
                                    k0 = 4 * (t - 8)
                                    scalar.wait_ge(c_fmm.h, k0 + 4)
                                    for k in range(k0, k0 + 4):
                                        c_fold.inc(scalar.copy(qkwT[:, k // 16, k % 16, :],
                                                               ps[k % 4][:, 0:128]))
                                        if k % 16 == 15 and k // 16 + 2 < 8:
                                            j = k // 16 + 2
                                            dqk[j % 2].dinc(scalar.dma_start(
                                                out=qkf[:, j % 2, :], in_=qk_tiles[j][0]))
                            scalar.wait_ge(c_lnx.h, NT_LN)
                            dxn[(NT_LN - 1) % 2].dinc(scalar.dma_start(
                                out=ln_dsts[NT_LN - 1], in_=xnbuf[:, (NT_LN - 1) % 2, :]))

                        @blk.vector
                        def _(vector):
                            if STOP < 1:
                                return

                            def emit_xn(t):
                                w = t % 2
                                vector.wait_ge(c_lns.h, t + 1)
                                if t >= 2:
                                    vector.wait_ge(dxn[w].h, 16 * (t // 2))
                                c_lnx.inc(vector.tensor_scalar(
                                    out=xnbuf[:, w, :], in0=xbuf[:, t % 3, :],
                                    scalar1=negm[:, w:w + 1], scalar2=rstdln[:, w:w + 1],
                                    op0=ALU.add, op1=ALU.mult))
                                cast_ops(vector, t, t + 1, cwf, cwb)

                            for t in range(NT_LN):
                                w = t % 2
                                vector.wait_ge(dx[t % 3].h, 16 * (t // 3 + 1))
                                bV.inc(vector.reduce_sum(mraw[:, w:w + 1], xbuf[:, t % 3, :],
                                                         axis=AXL.X))
                                vector.wait_ge(bV.h, bV.v)
                                bV.inc(vector.tensor_scalar_mul(negm[:, w:w + 1],
                                                                mraw[:, w:w + 1], -1.0 / 2048.0))
                                bV.inc(vector.scalar_tensor_tensor(
                                    out=m2[:, w:w + 1], in0=mraw[:, w:w + 1],
                                    scalar=1.0 / (2048.0 * 2048.0), in1=mraw[:, w:w + 1],
                                    op0=ALU.mult, op1=ALU.mult))
                                vector.wait_ge(bV.h, bV.v)
                                vector.wait_ge(c_sx2.h, t + 1)
                                if t >= 2:
                                    vector.wait_ge(c_lns.h, t - 1)
                                c_var.inc(vector.tensor_scalar(
                                    out=varbuf[:, w:w + 1], in0=sx2[:, w:w + 1],
                                    scalar1=1.0 / 2048.0, scalar2=m2[:, w:w + 1],
                                    op0=ALU.mult, op1=ALU.subtract))
                                if t >= 1:
                                    emit_xn(t - 1)
                            emit_xn(NT_LN - 1)

                        @blk.sync
                        def _(sync):
                            if STOP < 1:
                                return
                            for sl in range(8):
                                cast_dma_steps(sync, 4 * sl, 4 * (sl + 1), cwf, cwb)
                                sync.wait_ge(dxn[0].h, 16 * (2 * sl + 2))
                                sync.wait_ge(dxn[1].h, 16 * (2 * sl + 2))
                                xn_strip_steps(sync, sl, tbuf)
                            cast_dma_steps(sync, 32, CAST_B1, cwf, cwb)


                with ExitStack() as stB3:
                    E3 = stB3.enter_context
                    xnT = E3(nc.sbuf_tensor("xnT", [128, 16, 512], BF16))
                    wvT = E3(nc.sbuf_tensor("wvT", [128, 16, 512], BF16))
                    q2bf = E3(nc.sbuf_tensor("q2bf", [128, 4, 512], BF16))
                    qraw = E3(nc.sbuf_tensor("qraw", [128, 4, 512], F32))
                    rstdq = E3(nc.sbuf_tensor("rstdq", [1, 1024], F32))

                    with nc.Block() as blk:
                        @blk.sync
                        def _(sync):
                            if STOP < 2:
                                return
                            sync.wait_ge(dxn[0].h, 16 * (NT_LN // 2))
                            sync.wait_ge(dxn[1].h, 16 * (NT_LN // 2))
                            sync.wait_ge(c_fold.h, 128)
                            wco_wait(sync, CAST_WV)
                            sync.wait_ge(dtwx.h, 16 * 128)
                            strip_steps(sync, 0, WV_S, tbuf)
                            tw_wait(sync, WV_S)
                            d_wvt.dinc(sync.dma_start(
                                out=wvT[:, :, :],
                                in_=bass.AP(wvt_scr, 0, [[512, 128], [128 * 512, 16], [1, 512]])))
                            for s in range(8):
                                if s >= 1:
                                    sync.wait_ge(c_vfin.h, 4 * s)
                                sync.wait_ge(dtwx.h, 256 * (s + 1))
                                d_xnt.dinc(sync.dma_start(
                                    out=xnT[:, :, :],
                                    in_=bass.AP(xnt_scr, 512 * s,
                                                [[T, 128], [128 * T, 16], [1, 512]])))
                                cast_dma_steps(sync, CAST_B1 + 32 * s, CAST_B1 + 32 * (s + 1), cwf, cwb)

                        @blk.tensor
                        def _(tensor):
                            if STOP < 2:
                                return
                            tensor.wait_ge(c_fold.h, 128)
                            tensor.wait_ge(d_wvt.h, 16)
                            for s in range(8):
                                tensor.wait_ge(d_xnt.h, 16 * (s + 1))
                                for hm in range(8):
                                    idx = 8 * s + hm
                                    bank = ps[hm % 4]
                                    if idx >= 4:
                                        tensor.wait_ge(c_sq.h, idx - 3)
                                    if s >= 1:
                                        tensor.wait_ge(c_vev.h, 8 * s)
                                    last = None
                                    for fc in range(16):
                                        last = tensor.matmul(bank[:, 0:512], qkwT[:, hm, fc, :],
                                                             xnT[:, fc, :],
                                                             start=(fc == 0), stop=(fc == 15))
                                    c_pmm.inc(last)
                                    if hm in (3, 7):
                                        W = 2 * s + (hm // 4)
                                        jb = idx - 3
                                        for half in range(2):
                                            j0 = jb + 2 * half
                                            for j in (j0, j0 + 1):
                                                tensor.wait_ge(c_sq.h, j + 1)
                                                if j % 2 == 0 and j >= 2:
                                                    tensor.wait_ge(c_mev.h, j - 1)
                                                if j % 2 == 1 and j >= 3:
                                                    tensor.wait_ge(c_qhat.h, j - 2)
                                                c_msq.inc(tensor.matmul(
                                                    ps[4 + j % 2][0:1, 0:512], ones_col_bf[:, 0:1],
                                                    q2bf[:, j % 4, :], start=True, stop=True))
                                            tensor.wait_ge(c_rstd.h, 2 * W + half + 1)
                                            for j in (j0, j0 + 1):
                                                if j % 2 == 0:
                                                    tensor.wait_ge(c_mev.h, j + 2)
                                                elif j >= 2:
                                                    tensor.wait_ge(c_qhat.h, j - 1)
                                                c_bcast.inc(tensor.matmul(
                                                    ps[5 + j % 2][:, 0:512], ones_row_f[0:1, :],
                                                    rstdq[0:1, 512 * (j % 2):512 * (j % 2 + 1)],
                                                    start=True, stop=True))
                                tensor.wait_ge(c_qhat.h, 8 * (s + 1))
                                for tsub in range(4):
                                    bank = ps[tsub]
                                    if s >= 1:
                                        tensor.wait_ge(c_vev.h, 8 * (s - 1) + 2 * (tsub + 1))
                                    last = None
                                    for fc in range(16):
                                        last = tensor.matmul(
                                            bank[:, 0:512],
                                            xnT[:, fc, 128 * tsub:128 * (tsub + 1)],
                                            wvT[:, fc, :], start=(fc == 0), stop=(fc == 15))
                                    c_vfin.inc(last)

                        @blk.scalar
                        def _(scalar):
                            if STOP < 2:
                                return
                            for s in range(8):
                                for hm in range(8):
                                    idx = 8 * s + hm
                                    scalar.wait_ge(c_pmm.h, idx + 1)
                                    if idx >= 4:
                                        scalar.wait_ge(c_msq.h, idx - 3)
                                        scalar.wait_ge(c_qhat.h, idx - 3)
                                    scalar.activation(out=q2bf[:, idx % 4, :],
                                                      in_=ps[hm % 4][:, 0:512], func=AF.Square)
                                    c_sq.inc(scalar.copy(qraw[:, idx % 4, :], ps[hm % 4][:, 0:512]))
                                    if hm in (3, 7):
                                        W = 2 * s + (hm // 4)
                                        jb = idx - 3
                                        for half in range(2):
                                            j0 = jb + 2 * half
                                            for j in (j0, j0 + 1):
                                                scalar.wait_ge(c_msq.h, j + 1)
                                                if j >= 2:
                                                    scalar.wait_ge(c_bcast.h, j - 1)
                                                c_mev.inc(scalar.copy(
                                                    rstdq[0:1, 512 * (j % 2):512 * (j % 2 + 1)],
                                                    ps[4 + j % 2][0:1, 0:512]))
                                            scalar.wait_ge(c_mev.h, j0 + 2)
                                            bS.inc(scalar.activation(
                                                out=rstdq[0:1, :], in_=rstdq[0:1, :],
                                                func=AF.Ln, bias=eps_rms[0:1, 0:1],
                                                scale=1.0 / 128.0))
                                            scalar.wait_ge(bS.h, bS.v)
                                            c_rstd.inc(scalar.activation(
                                                out=rstdq[0:1, :], in_=rstdq[0:1, :],
                                                func=AF.Exp, scale=-0.5,
                                                bias=(qbias[0:1, 0:1] if half == 0
                                                      else zbias[0:1, 0:1])))

                        @blk.vector
                        def _(vector):
                            if STOP < 2:
                                return
                            vector.wait_ge(dxn[0].h, 16 * (NT_LN // 2))
                            vector.wait_ge(dxn[1].h, 16 * (NT_LN // 2))
                            vector.memset(v_sb[:, :, :, 256:257], 1.0)
                            for s in range(8):
                                for hm in range(8):
                                    idx = 8 * s + hm
                                    vector.wait_ge(c_bcast.h, idx + 1)
                                    c_qhat.inc(vector.tensor_mul(
                                        qk_sb[:, hm, 512 * s:512 * (s + 1)],
                                        qraw[:, idx % 4, :], ps[5 + idx % 2][:, 0:512]))
                                for tsub in range(4):
                                    vector.wait_ge(c_vfin.h, 4 * s + tsub + 1)
                                    bank = ps[tsub]
                                    c_vev.inc(vector.tensor_copy(v_sb[:, 0, 4 * s + tsub, 0:256],
                                                                 bank[:, 0:256]))
                                    c_vev.inc(vector.tensor_copy(v_sb[:, 1, 4 * s + tsub, 0:256],
                                                                 bank[:, 256:512]))
                                cast_ops(vector, CAST_B1 + 32 * s, CAST_B1 + 32 * (s + 1), cwf, cwb)

                with ExitStack() as stB4:
                    E4 = stB4.enter_context
                    masks_sb = E4(nc.sbuf_tensor("masks_sb", [128, 4, 512], BF16))
                    p_ring = E4(nc.sbuf_tensor("p_ring", [128, 3, 512], BF16))
                    tmp1 = E4(nc.sbuf_tensor("tmp1", [128, 1024], F32))
                    ybf = E4(nc.sbuf_tensor("ybf", [128, 2, 4, 256], BF16))
                    rc = E4(nc.sbuf_tensor("rc", [128, 4], F32))
                    nl2 = E4(nc.sbuf_tensor("nl2", [128, 4], F32))
                    tbuf4 = E4(nc.sbuf_tensor("tbuf4", [128, 2, 4096], BF16, is_transpose_dest=True))

                    with nc.Block() as blk:
                        @blk.tensor
                        def _(tensor):
                            if STOP < 3:
                                return
                            tensor.wait_ge(c_qhat.h, 64)
                            tensor.wait_ge(c_vev.h, 64)

                            def emit_pv(tensor, hh, i, s, tt, g, Ts):
                                boundary = tt >= 4 * s
                                last = None
                                for q4 in range(4):
                                    if q4 == 0:
                                        tensor.wait_ge(c_exp.h, exp_idx[(hh, i, s, tt)])
                                        if boundary:
                                            tensor.wait_ge(c_msk.h, msk_idx[(hh, i, s, tt)])
                                        if tt == 0:
                                            tensor.wait_ge(c_oev.h, 4 * (g - 1))
                                    last = tensor.matmul(
                                        ps[q4][:, 0:257],
                                        p_ring[:, tt % 3, 128 * q4:128 * (q4 + 1)],
                                        v_sb[:, hh, tt, :],
                                        start=(tt == 0), stop=(tt == Ts - 1))
                                if tt == Ts - 1:
                                    c_pv.inc(last)
                                else:
                                    c_pvt.inc(last)

                            for hh in range(2):
                                for s in range(8):
                                    for i in range(2):
                                        qhm = 4 * hh + i
                                        khm = 4 * hh + 2 + i
                                        g = grp_idx[(hh, i, s)]
                                        Ts = tkc(s)
                                        for tt in range(Ts):
                                            if tt >= 2:
                                                tensor.wait_ge(c_exp.h, exp_idx[(hh, i, s, tt - 2)])
                                            elif g >= 2:
                                                tensor.wait_ge(c_exp.h, EXPCUM[g - 1])
                                            c_sc.inc(tensor.matmul(
                                                ps[4 + tt % 2][:, 0:512],
                                                qk_sb[:, khm, 128 * tt:128 * (tt + 1)],
                                                qk_sb[:, qhm, 512 * s:512 * (s + 1)],
                                                start=True, stop=True))
                                            if tt >= 1:
                                                emit_pv(tensor, hh, i, s, tt - 1, g, Ts)
                                        emit_pv(tensor, hh, i, s, Ts - 1, g, Ts)

                        @blk.scalar
                        def _(scalar):
                            if STOP < 3:
                                return
                            for hh in range(2):
                                for s in range(8):
                                    for i in range(2):
                                        g = grp_idx[(hh, i, s)]
                                        Ts = tkc(s)
                                        for tt in range(Ts):
                                            scalar.wait_ge(c_sc.h, exp_idx[(hh, i, s, tt)])
                                            if tt >= 3:
                                                scalar.wait_ge(c_pvt.h, pvt_idx[(hh, i, s, tt - 3)])
                                            elif g >= 2:
                                                scalar.wait_ge(c_pv.h, g - 1)
                                            c_exp.inc(scalar.activation(
                                                out=p_ring[:, tt % 3, :],
                                                in_=ps[4 + tt % 2][:, 0:512], func=AF.Exp))

                        @blk.vector
                        def _(vector):
                            if STOP < 3:
                                return
                            vector.wait_ge(c_lam.h, LAM_DONE)
                            vector.wait_ge(d_msk.h, 16)
                            for hh in range(2):
                                for s in range(8):
                                    for i in range(2):
                                        g = grp_idx[(hh, i, s)]
                                        for tt in range(4 * s, 4 * s + 4):
                                            vector.wait_ge(c_exp.h, exp_idx[(hh, i, s, tt)])
                                            c_msk.inc(vector.tensor_mul(
                                                p_ring[:, tt % 3, :], p_ring[:, tt % 3, :],
                                                masks_sb[:, tt - 4 * s, :]))
                                        vector.wait_ge(c_pv.h, g)
                                        if i == 0:
                                            for q4 in range(4):
                                                inst = vector.reciprocal(rc[:, q4:q4 + 1],
                                                                         ps[q4][:, 256:257])
                                            c_rec.inc(inst)
                                            vector.wait_ge(c_rec.h, c_rec.v)
                                            for q4 in range(4):
                                                c_oev.inc(vector.tensor_scalar_mul(
                                                    tmp1[:, 256 * q4:256 * (q4 + 1)],
                                                    ps[q4][:, 0:256], rc[:, q4:q4 + 1]))
                                        else:
                                            for q4 in range(4):
                                                inst = vector.reciprocal(rc[:, q4:q4 + 1],
                                                                         ps[q4][:, 256:257])
                                            c_rec.inc(inst)
                                            vector.wait_ge(c_rec.h, c_rec.v)
                                            for q4 in range(4):
                                                inst = vector.tensor_mul(nl2[:, q4:q4 + 1],
                                                                         rc[:, q4:q4 + 1],
                                                                         neglam[:, 0:1])
                                            c_rec.inc(inst)
                                            vector.wait_ge(c_rec.h, c_rec.v)
                                            vector.wait_ge(c_oev.h, 4 * grp_idx[(hh, 0, s)])
                                            widx = 8 * hh + s + 1
                                            if widx >= 3:
                                                vector.wait_ge(dy[widx % 2].h,
                                                               64 * ((widx - 1) // 2))
                                            for q4 in range(4):
                                                c_oev.inc(vector.scalar_tensor_tensor(
                                                    out=ybf[:, s % 2, q4, :],
                                                    in0=ps[q4][:, 0:256],
                                                    scalar=nl2[:, q4:q4 + 1],
                                                    in1=tmp1[:, 256 * q4:256 * (q4 + 1)],
                                                    op0=ALU.mult, op1=ALU.add))
                                            gi = 8 * hh + s
                                            cast_ops(vector, CAST_B3 + 8 * gi,
                                                     CAST_B3 + 8 * (gi + 1), cwf, cwb)


                        @blk.sync
                        def _(sync):
                            if STOP < 3:
                                return
                            tw_wait(sync, WV_S)
                            sync.wait_ge(c_vfin.h, 32)
                            sync.wait_ge(c_qhat.h, 64)
                            sync.wait_ge(c_msq.h, 64)
                            d_msk.dinc(sync.dma_start(out=masks_sb[:, :, :], in_=masks_in[:, :, :]))
                            for hh in range(2):
                                for s in range(8):
                                    widx = 8 * hh + s + 1
                                    sync.wait_ge(c_oev.h, 4 * grp_idx[(hh, 1, s)])
                                    for q4 in range(4):
                                        dy[widx % 2].dinc(sync.dma_start(
                                            out=ycc_in[512 * s + 128 * q4:512 * s + 128 * (q4 + 1),
                                                       256 * hh:256 * (hh + 1)],
                                            in_=ybf[:, s % 2, q4, :]))
                                    gi = 8 * hh + s
                                    if gi == 0:
                                        wco_wait(sync, CAST_W1)
                                    if gi == 8:
                                        wco_wait(sync, CAST_W3)
                                    strip_steps(sync, WV_S + 4 * gi, WV_S + 4 * (gi + 1), tbuf4)
                                    cast_dma_steps(sync, CAST_B3 + 8 * gi,
                                                   CAST_B3 + 8 * (gi + 1), cwf, cwb)

                        @blk.gpsimd
                        def _(gpsimd):
                            if STOP < 3:
                                return
                            gpsimd.wait_ge(dy[0].h, 64 * 8)
                            gpsimd.wait_ge(dy[1].h, 64 * 8)
                            c_cc.inc(gpsimd.collective_compute(
                                "AllGather", ALU.bypass,
                                replica_groups=[[0, 1, 2, 3], [4, 5, 6, 7]],
                                ins=[ycc_in.ap().opt()], outs=[ycc_out.ap().opt()]))

            with ExitStack() as stB5:
                E5 = stB5.enter_context
                tbuf5 = E5(nc.sbuf_tensor("tbuf5", [128, 2, 4096], BF16, is_transpose_dest=True))
                ysl2 = E5(nc.sbuf_tensor("ysl2", [128, 2, 8, 512], BF16))
                yT = E5(nc.sbuf_tensor("yT", [128, 16, 1024], BF16, is_transpose_dest=True))
                woT = E5(nc.sbuf_tensor("woT", [128, 16, 2048], BF16, is_transpose_dest=True))
                attstg = E5(nc.sbuf_tensor("attstg", [128, 2, 512], F32))

                with nc.Block() as blk:
                    @blk.sync
                    def _(sync):
                        if STOP < 4:
                            return
                        wco_wait(sync, CAST_WO)
                        for yc in range(16):
                            d_wot.dinc(sync.dma_start_transpose(
                                woT[:, yc, :], wo_scr[0:2048, 128 * yc:128 * (yc + 1)]))
                        wco_wait(sync, CAST_W2)
                        strip_steps(sync, W3_S, W3_S + 32, tbuf5)
                        sync.wait_ge(dysw[0].h, 32)
                        sync.wait_ge(dysw[1].h, 32)
                        for r in range(4):
                            for sub in range(4):
                                d_yt2.dinc(sync.dma_start_transpose(
                                    yT[:, 4 * r + sub, :],
                                    ysl_scr[TS * r:TS * (r + 1), 128 * sub:128 * (sub + 1)]))
                        strip_steps(sync, W3_S + 32, W2_S, tbuf5)

                    @blk.gpsimd
                    def _(gpsimd):
                        if STOP < 4:
                            return
                        gpsimd.wait_ge(c_cc.h, 1)
                        gpsimd.wait_ge(d_c.h, d_c.v)
                        for r in range(4):
                            if r >= 2:
                                gpsimd.wait_ge(dysw[r % 2].h, 16 * (r // 2))
                            for tt in range(8):
                                dycc[r % 2].dinc(gpsimd.indirect_dma_start(
                                    out=ysl2[:, r % 2, tt, :], out_offset=None,
                                    in_=ycc_out[:, :],
                                    in_offset=bass.IndirectOffsetOnAxis(
                                        ap=jidx_sb[:, 8 * r + tt:8 * r + tt + 1], axis=0)))
                            gpsimd.wait_ge(dycc[r % 2].h, 128 * (r // 2 + 1))
                            ap = bass.AP(ysl_scr, r * TS * 512,
                                         [[512, 128], [128 * 512, 8], [1, 512]])
                            dysw[r % 2].dinc(gpsimd.dma_start(out=ap, in_=ysl2[:, r % 2, :, :]))

                    @blk.tensor
                    def _(tensor):
                        if STOP < 4:
                            return
                        tensor.wait_ge(d_yt2.h, 16 * 16)
                        tensor.wait_ge(d_wot.h, 16 * 16)
                        for dc in range(16):
                            for th in range(2):
                                idx = 2 * dc + th
                                bank = ps[4 + idx % 2]
                                if idx >= 2:
                                    tensor.wait_ge(c_woev.h, idx - 1)
                                last = None
                                for yc in range(16):
                                    last = tensor.matmul(
                                        bank[:, 0:512], woT[:, yc, 128 * dc:128 * (dc + 1)],
                                        yT[:, yc, 512 * th:512 * (th + 1)],
                                        start=(yc == 0), stop=(yc == 15))
                                c_wo.inc(last)

                    @blk.vector
                    def _(vector):
                        if STOP < 4:
                            return
                        for idx in range(32):
                            vector.wait_ge(c_wo.h, idx + 1)
                            if idx >= 2:
                                vector.wait_ge(datt[idx % 2].h, 16 * (idx // 2))
                            c_woev.inc(vector.tensor_copy(attstg[:, idx % 2, :],
                                                          ps[4 + idx % 2][:, 0:512]))

                    @blk.scalar
                    def _(scalar):
                        if STOP < 4:
                            return
                        scalar.wait_ge(c_cc.h, 1)
                        for dc in range(16):
                            for th in range(2):
                                idx = 2 * dc + th
                                scalar.wait_ge(c_woev.h, idx + 1)
                                datt[idx % 2].dinc(scalar.dma_start(
                                    out=attn_scr[128 * dc:128 * (dc + 1), 512 * th:512 * (th + 1)],
                                    in_=attstg[:, idx % 2, :]))

        with ExitStack() as st6:
            E6 = st6.enter_context
            xnTs = E6(nc.sbuf_tensor("xnTs", [128, 16, 1024], BF16, is_transpose_dest=True))
            hidden = E6(nc.sbuf_tensor("hidden", [128, 72, 512], BF16))
            w1w = E6(nc.sbuf_tensor("w1w", [128, 2, 16, 256], BF16))
            w3w = E6(nc.sbuf_tensor("w3w", [128, 2, 16, 256], BF16))
            w2t = E6(nc.sbuf_tensor("w2t", [128, 2, 2048], BF16))
            attb = E6(nc.sbuf_tensor("attb", [128, 2, 512], F32))
            sumfm = E6(nc.sbuf_tensor("sumfm", [128, 16, 512], F32))
            xres = E6(nc.sbuf_tensor("xres", [128, 2048], F32))
            outstg = E6(nc.sbuf_tensor("outstg", [128, 2048], F32))
            silu_s = E6(nc.sbuf_tensor("silu_s", [128, 2, 512], F32))
            ps7 = E6(nc.psum_tensor("ps7", [128, 512], F32))

            def butterfly_sched():
                sm = list(range(64))
                pool = list(range(64, 72))
                freed = []
                opsl = []
                npair = 0
                for k in range(6):
                    for a in range(64):
                        if a & (1 << k):
                            continue
                        bpair = a | (1 << k)
                        bar = (npair % 4 == 0) and npair > 0
                        if npair % 4 == 0 and npair > 0:
                            pool.extend(freed)
                            freed = []
                        s0 = pool.pop(0)
                        s1 = pool.pop(0)
                        opsl.append((s0, sm[a], sm[bpair], True, bar))
                        opsl.append((s1, sm[a], sm[bpair], False, False))
                        freed.extend([sm[a], sm[bpair]])
                        sm[a], sm[bpair] = s0, s1
                        npair += 1
                return opsl, sm

            BF_OPS, BF_SLOT = butterfly_sched()

            with nc.Block() as blk:
                @blk.sync
                def _(sync):
                    if STOP < 5:
                        return
                    sync.wait_ge(c_cast.h, N_CAST)
                    sync.wait_ge(c_wo.h, 32)
                    sync.wait_ge(datt[0].h, 256)
                    sync.wait_ge(datt[1].h, 256)
                    tw_wait(sync, W2_S)
                    for fc in range(16):
                        d_xnmt.dinc(sync.dma_start_transpose(
                            xnTs[:, fc, :], xnm_scr[0:TS, 128 * fc:128 * (fc + 1)]))
                    for p in range(2):
                        for hw in range(32):
                            gw = 32 * p + hw
                            if gw >= 2:
                                sync.wait_ge(c_gl.h, 4 * (gw - 1))
                            for (wt, buf) in ((w1t_scr, w1w), (w3t_scr, w3w)):
                                dw13[gw % 2].dinc(sync.dma_start(
                                    out=buf[:, gw % 2, :, :],
                                    in_=bass.AP(wt, 256 * hw,
                                                [[H, 128], [128 * H, 16], [1, 256]])))
                        for dcg in range(2):
                            bblk = 2 * p + dcg
                            for hc in range(64):
                                widx = bblk * 64 + hc
                                if widx >= 2:
                                    if hc == 1:
                                        sync.wait_ge(c_w2.h, 8 * bblk)
                                    elif hc == 0:
                                        sync.wait_ge(c_w2h.h, 63 * bblk)
                                    else:
                                        sync.wait_ge(c_w2h.h, 63 * bblk + hc - 1)
                                dw2t[widx % 2].dinc(sync.dma_start(
                                    out=w2t[:, hc % 2, :],
                                    in_=w2t_scr[128 * hc:128 * (hc + 1), 0:2048]))

                @blk.scalar
                def _(scalar):
                    if STOP < 5:
                        return
                    scalar.wait_ge(datt[0].h, 256)
                    scalar.wait_ge(datt[1].h, 256)
                    for p in range(2):
                        for ch in range(64):
                            gg = 64 * p + ch
                            scalar.wait_ge(c_gl.h, 2 * gg + 1)
                            if gg >= 2:
                                scalar.wait_ge(c_hid.h, gg - 1)
                            c_sil.inc(scalar.activation(out=silu_s[:, ch % 2, :],
                                                        in_=ps[ch % 2][:, 0:512], func=AF.Silu))
                        for dcg in range(2):
                            for dcl in range(8):
                                dc = 8 * dcg + dcl
                                aidx = (2 * p + dcg) * 8 + dcl
                                if aidx >= 2:
                                    scalar.wait_ge(c_w2e.h, aidx - 1)
                                datr[aidx % 2].dinc(scalar.dma_start(
                                    out=attb[:, dcl % 2, :],
                                    in_=attn_scr[128 * dc:128 * (dc + 1), 512 * p:512 * (p + 1)]))
                        for sub in range(4):
                            bidx = 4 * p + sub
                            if bidx >= 1:
                                scalar.wait_ge(c_fin.h, 16 * bidx)
                            d_xr.dinc(scalar.dma_start(
                                out=xres[:, :],
                                in_=x_mine[512 * p + 128 * sub:512 * p + 128 * (sub + 1), :]))
                            scalar.wait_ge(c_fin.h, 16 * (bidx + 1))
                            d_out.dinc(scalar.dma_start(
                                out=out[512 * p + 128 * sub:512 * p + 128 * (sub + 1), :],
                                in_=outstg[:, :]))

                @blk.tensor
                def _(tensor):
                    if STOP < 5:
                        return
                    tensor.wait_ge(c_woev.h, 32)
                    tensor.wait_ge(d_xnmt.h, 16 * 16)
                    for p in range(2):
                        if p == 1:
                            tensor.wait_ge(c_w2e.h, 16)
                        for hw in range(32):
                            gw = 32 * p + hw
                            tensor.wait_ge(dw13[gw % 2].h, 32 * (gw // 2 + 1))
                            for hcl in range(2):
                                ch = 2 * hw + hcl
                                gg = 64 * p + ch
                                gbank = ps[ch % 2]
                                lbank = ps[2 + ch % 2]
                                if gg >= 2:
                                    tensor.wait_ge(c_sil.h, gg - 1)
                                last = None
                                for fc in range(16):
                                    last = tensor.matmul(
                                        gbank[:, 0:512],
                                        w1w[:, gw % 2, fc, 128 * hcl:128 * (hcl + 1)],
                                        xnTs[:, fc, 512 * p:512 * (p + 1)],
                                        start=(fc == 0), stop=(fc == 15))
                                c_gl.inc(last)
                                if gg >= 2:
                                    tensor.wait_ge(c_hid.h, gg - 1)
                                last = None
                                for fc in range(16):
                                    last = tensor.matmul(
                                        lbank[:, 0:512],
                                        w3w[:, gw % 2, fc, 128 * hcl:128 * (hcl + 1)],
                                        xnTs[:, fc, 512 * p:512 * (p + 1)],
                                        start=(fc == 0), stop=(fc == 15))
                                c_gl.inc(last)
                        if p == 1:
                            tensor.wait_ge(c_fin.h, 64)
                        for ch in range(64):
                            tensor.wait_ge(c_hid.h, 64 * p + ch + 1)
                            if ch >= 2:
                                tensor.wait_ge(c_hev.h, 64 * p + ch - 1)
                            c_h128.inc(tensor.matmul(ps[4 + ch % 2][:, 0:512], h128_sb[:, :],
                                                     hidden[:, ch, :], start=True, stop=True))
                        tensor.wait_ge(c_bf.h, p + 1)
                        for dcg in range(2):
                            bblk = 2 * p + dcg
                            for hc in range(64):
                                widx = bblk * 64 + hc
                                tensor.wait_ge(dw2t[widx % 2].h, 16 * (widx // 2 + 1))
                                if hc == 0:
                                    tensor.wait_ge(c_w2e.h, 8 * bblk)
                                last = None
                                for dcl in range(8):
                                    bank = ps[dcl] if dcl < 7 else ps7
                                    last = tensor.matmul(
                                        bank[:, 0:512],
                                        w2t[:, hc % 2, 128 * (8 * dcg + dcl):128 * (8 * dcg + dcl + 1)],
                                        hidden[:, BF_SLOT[hc], :],
                                        start=(hc == 0), stop=(hc == 63))
                                    if hc == 63:
                                        c_w2.inc(last)
                                if hc < 63:
                                    c_w2h.inc(last)
                        tensor.wait_ge(c_w2e.h, 16 * (p + 1))
                        for sub in range(4):
                            for dc in range(16):
                                cnt = (4 * p + sub) * 16 + dc + 1
                                if cnt >= 3:
                                    tensor.wait_ge(c_fin.h, cnt - 2)
                                c_tr.inc(tensor.transpose(
                                    ps[4 + cnt % 2][:, 0:128],
                                    sumfm[:, dc, 128 * sub:128 * (sub + 1)],
                                    identf_sb[:, :]))

                @blk.vector
                def _(vector):
                    if STOP < 5:
                        return
                    for p in range(2):
                        for ch in range(64):
                            gg = 64 * p + ch
                            vector.wait_ge(c_gl.h, 2 * gg + 2)
                            vector.wait_ge(c_sil.h, gg + 1)
                            c_hid.inc(vector.tensor_mul(hidden[:, ch, :],
                                                        ps[2 + ch % 2][:, 0:512],
                                                        silu_s[:, ch % 2, :]))
                        for ch in range(64):
                            vector.wait_ge(c_h128.h, 64 * p + ch + 1)
                            c_hev.inc(vector.tensor_scalar_mul(hidden[:, ch, :],
                                                               ps[4 + ch % 2][:, 0:512], FW_SCALE))
                        vector.wait_ge(c_hev.h, 64 * (p + 1))
                        inst = None
                        for (dst, a, bsl, is_add, stage_first) in BF_OPS:
                            if stage_first and inst is not None:
                                bV.inc(inst)
                                vector.wait_ge(bV.h, bV.v)
                            if is_add:
                                inst = vector.tensor_add(hidden[:, dst, :], hidden[:, a, :],
                                                         hidden[:, bsl, :])
                            else:
                                inst = vector.tensor_sub(hidden[:, dst, :], hidden[:, a, :],
                                                         hidden[:, bsl, :])
                        c_bf.inc(inst)
                        for dcg in range(2):
                            for dcl in range(8):
                                dc = 8 * dcg + dcl
                                aidx = (2 * p + dcg) * 8 + dcl
                                vector.wait_ge(c_w2.h, 8 * (2 * p + dcg) + dcl + 1)
                                vector.wait_ge(datr[aidx % 2].h, 16 * (aidx // 2 + 1))
                                bank = ps[dcl] if dcl < 7 else ps7
                                c_w2e.inc(vector.tensor_add(sumfm[:, dc, :], bank[:, 0:512],
                                                            attb[:, dcl % 2, :]))
                        for sub in range(4):
                            bidx = 4 * p + sub
                            vector.wait_ge(d_xr.h, 16 * (bidx + 1))
                            if bidx >= 1:
                                vector.wait_ge(d_out.h, 16 * bidx)
                            for dc in range(16):
                                cnt = (4 * p + sub) * 16 + dc + 1
                                vector.wait_ge(c_tr.h, cnt)
                                c_fin.inc(vector.tensor_add(outstg[:, 128 * dc:128 * (dc + 1)],
                                                            ps[4 + cnt % 2][:, 0:128],
                                                            xres[:, 128 * dc:128 * (dc + 1)]))

                @blk.gpsimd
                def _(gpsimd):
                    if STOP < 5:
                        return
                    gpsimd.wait_ge(d_out.h, 16 * 8)

    return nc


_NC_CACHE = None


def _get_nc():
    global _NC_CACHE
    if _NC_CACHE is None:
        _NC_CACHE = build_program()
    return _NC_CACHE


def kernel(L0_attn_x, L0_attn_wq, L0_attn_wk, L0_attn_wv, L0_attn_wo,
           L0_attn_lq1, L0_attn_lk1, L0_attn_lq2, L0_attn_lk2,
           L0_ffn_W1, L0_ffn_W2, L0_ffn_W3):
    x = np.ascontiguousarray(np.asarray(L0_attn_x, dtype=np.float32))
    wq = np.ascontiguousarray(np.asarray(L0_attn_wq, dtype=np.float32))
    wk = np.ascontiguousarray(np.asarray(L0_attn_wk, dtype=np.float32))
    wv = np.ascontiguousarray(np.asarray(L0_attn_wv, dtype=np.float32))
    wo = np.ascontiguousarray(np.asarray(L0_attn_wo, dtype=np.float32))
    w1 = np.ascontiguousarray(np.asarray(L0_ffn_W1, dtype=np.float32))
    w2 = np.ascontiguousarray(np.asarray(L0_ffn_W2, dtype=np.float32))
    w3 = np.ascontiguousarray(np.asarray(L0_ffn_W3, dtype=np.float32))
    lqk = [np.ascontiguousarray(np.asarray(v, dtype=np.float32).reshape(2, 128).T)
           for v in (L0_attn_lq1, L0_attn_lk1, L0_attn_lq2, L0_attn_lk2)]
    masks, h128, ropePT, identb, identf = host_consts()

    def jidx_host(j):
        jj = np.zeros((128, 32), np.uint32)
        for r in range(4):
            for tt in range(8):
                jj[:, 8 * r + tt] = r * T + TS * j + 128 * tt + np.arange(128)
        return jj

    nc = _get_nc()
    in_maps = []
    for c in range(8):
        b, j = divmod(c, 4)
        in_maps.append({
            "x_full": x[b],
            "x_mine": np.ascontiguousarray(x[b, TS * j:TS * (j + 1)]),
            "wq_s": np.ascontiguousarray(wq[512 * j:512 * (j + 1)]),
            "wk_s": np.ascontiguousarray(wk[512 * j:512 * (j + 1)]),
            "wv_s": np.ascontiguousarray(wv[512 * j:512 * (j + 1)]),
            "wo_in": wo, "w1_in": w1, "w3_in": w3, "w2_in": w2,
            "lq1_in": lqk[0], "lk1_in": lqk[1], "lq2_in": lqk[2], "lk2_in": lqk[3],
            "masks_in": masks, "h128_in": h128,
            "ropept_in": np.ascontiguousarray(ropePT[2 * j:2 * j + 2].transpose(1, 0, 2)),
            "identb_in": identb, "identf_in": identf,
            "jidx_in": jidx_host(j),
        })
    res = run_bass_kernel_spmd(nc, in_maps, core_ids=list(range(8)))
    outp = np.empty((B, T, C), dtype=np.float32)
    for c in range(8):
        b, j = divmod(c, 4)
        outp[b, TS * j:TS * (j + 1), :] = res.results[c]["out"]
    return outp


# version 13
# speedup vs baseline: 1.0603x; 1.0603x over previous
import numpy as np
import ml_dtypes
from contextlib import ExitStack

import concourse.bass as bass
import concourse.mybir as mybir
from concourse.bass_utils import run_bass_kernel_spmd

F32 = mybir.dt.float32
BF16 = mybir.dt.bfloat16
U32 = mybir.dt.uint32
AF = mybir.ActivationFunctionType
ALU = mybir.AluOpType
AXL = mybir.AxisListType

B, T, C = 2, 4096, 2048
NH, HD, HH, H = 8, 256, 128, 8192
TS = 1024
LAM_INIT = 0.2
RMS_EPS = 1.1920929e-07
LN_EPS = 1e-5
FW_SCALE = float(np.float32(1.0) / np.float32(np.sqrt(np.float32(8192.0))))
ROPE_BASE = 10000.0

STOP_AFTER = 99


class Sem:
    def __init__(self, stack, nc, name):
        self.h = stack.enter_context(nc.semaphore(name))
        self.v = 0

    def inc(self, inst, n=1):
        inst.then_inc(self.h, n)
        self.v += n
        return inst

    def dinc(self, inst):
        inst.then_inc(self.h, 16)
        self.v += 16
        return inst


def host_consts():
    p = np.arange(128)[:, None, None]
    o = np.arange(4)[None, :, None]
    f = np.arange(512)[None, None, :]
    masks = (f >= p + 128 * o).astype(ml_dtypes.bfloat16)
    i = np.arange(128)
    pc = np.array([bin(v).count("1") for v in range(128)])
    h128 = ((-1.0) ** pc[(i[:, None] & i[None, :])]).astype(ml_dtypes.bfloat16)
    j64 = np.arange(0, HH, 2, dtype=np.float64) / HH
    inv_freq = (1.0 / ROPE_BASE) ** j64
    ropePT = np.zeros((NH, 128, 128), dtype=np.float64)
    for h in range(NH):
        cos = np.cos(h * inv_freq)
        sin = np.sin(h * inv_freq)
        P = np.zeros((128, 128))
        for d in range(64):
            P[d, d] = cos[d]
            P[d, d + 64] = sin[d]
            P[d + 64, d] = -sin[d]
            P[d + 64, d + 64] = cos[d]
        ropePT[h] = P.T
    return (masks, h128, ropePT.astype(np.float32),
            np.eye(128, dtype=ml_dtypes.bfloat16), np.eye(128, dtype=np.float32))


def build_program():
    nc = bass.Bass("TRN2", target_bir_lowering=False)

    x_full = nc.dram_tensor("x_full", [T, C], F32, kind="ExternalInput").ap()
    x_mine = nc.dram_tensor("x_mine", [TS, C], F32, kind="ExternalInput").ap()
    wq_s = nc.dram_tensor("wq_s", [512, C], F32, kind="ExternalInput").ap()
    wk_s = nc.dram_tensor("wk_s", [512, C], F32, kind="ExternalInput").ap()
    wv_s = nc.dram_tensor("wv_s", [512, C], F32, kind="ExternalInput").ap()
    wo_in = nc.dram_tensor("wo_in", [C, C], F32, kind="ExternalInput").ap()
    w1_in = nc.dram_tensor("w1_in", [H, C], F32, kind="ExternalInput").ap()
    w3_in = nc.dram_tensor("w3_in", [H, C], F32, kind="ExternalInput").ap()
    w2_in = nc.dram_tensor("w2_in", [C, H], F32, kind="ExternalInput").ap()
    lq1_in = nc.dram_tensor("lq1_in", [128, 2], F32, kind="ExternalInput").ap()
    lk1_in = nc.dram_tensor("lk1_in", [128, 2], F32, kind="ExternalInput").ap()
    lq2_in = nc.dram_tensor("lq2_in", [128, 2], F32, kind="ExternalInput").ap()
    lk2_in = nc.dram_tensor("lk2_in", [128, 2], F32, kind="ExternalInput").ap()
    masks_in = nc.dram_tensor("masks_in", [128, 4, 512], BF16, kind="ExternalInput").ap()
    h128_in = nc.dram_tensor("h128_in", [128, 128], BF16, kind="ExternalInput").ap()
    ropept_in = nc.dram_tensor("ropept_in", [128, 2, 128], F32, kind="ExternalInput").ap()
    identb_in = nc.dram_tensor("identb_in", [128, 128], BF16, kind="ExternalInput").ap()
    identf_in = nc.dram_tensor("identf_in", [128, 128], F32, kind="ExternalInput").ap()
    jidx_in = nc.dram_tensor("jidx_in", [128, 32], U32, kind="ExternalInput").ap()
    out = nc.dram_tensor("out", [TS, C], F32, kind="ExternalOutput").ap()

    xn_scr = nc.dram_tensor("xn_scr", [T, C], BF16)
    xnt_scr = nc.dram_tensor("xnt_scr", [C, T], BF16)
    xnm_scr = nc.dram_tensor("xnm_scr", [TS, C], BF16)
    wv_scr = nc.dram_tensor("wv_scr", [512, C], BF16)
    wvt_scr = nc.dram_tensor("wvt_scr", [C, 512], BF16)
    wo_scr = nc.dram_tensor("wo_scr", [C, C], BF16)
    w1_scr = nc.dram_tensor("w1_scr", [H, C], BF16)
    w1t_scr = nc.dram_tensor("w1t_scr", [C, H], BF16)
    w3_scr = nc.dram_tensor("w3_scr", [H, C], BF16)
    w3t_scr = nc.dram_tensor("w3t_scr", [C, H], BF16)
    w2_scr = nc.dram_tensor("w2_scr", [C, H], BF16)
    w2t_scr = nc.dram_tensor("w2t_scr", [H, C], BF16)
    ycc_in = nc.dram_tensor("ycc_in", [T, 512], BF16)
    ycc_out = nc.dram_tensor("ycc_out", [4 * T, 512], BF16)
    attn_scr = nc.dram_tensor("attn_scr", [C, TS], F32)
    ysl_scr = nc.dram_tensor("ysl_scr", [T, 512], BF16)

    st_all = ExitStack()
    with st_all:
        def S(name):
            return Sem(st_all, nc, name)

        d_c = S("d_c"); dqk = [S(f"dqk{i}") for i in range(2)]
        dx = [S(f"dx{i}") for i in range(3)]
        dxn = [S(f"dxn{i}") for i in range(2)]
        dwci = [S(f"dwci{i}") for i in range(2)]
        dwco = [S(f"dwco{i}") for i in range(2)]
        dtr = [S(f"dtr{i}") for i in range(2)]
        dtrx = S("dtrx"); dtwx = S("dtwx")
        dtw = [S(f"dtw{i}") for i in range(2)]
        d_xnt = S("d_xnt"); d_wvt = S("d_wvt"); d_wot = S("d_wot")
        dw13 = [S(f"dw13_{i}") for i in range(2)]
        dw2t = [S(f"dw2t{i}") for i in range(2)]
        dy = [S(f"dy{i}") for i in range(2)]
        dycc = [S(f"dycc{i}") for i in range(2)]
        dysw = [S(f"dysw{i}") for i in range(2)]
        d_yt2 = S("d_yt2")
        datt = [S(f"datt{i}") for i in range(2)]
        datr = [S(f"datr{i}") for i in range(2)]
        d_xr = S("d_xr"); d_xnmt = S("d_xnmt"); d_out = S("d_out"); d_msk = S("d_msk")
        c_cc = S("c_cc"); c_cast = S("c_cast"); c_lam = S("c_lam")
        c_var = S("c_var"); c_sx2 = S("c_sx2"); c_lns = S("c_lns"); c_lnx = S("c_lnx")
        c_fmm = S("c_fmm"); c_fold = S("c_fold")
        c_pmm = S("c_pmm"); c_sq = S("c_sq")
        c_msq = S("c_msq"); c_rstd = S("c_rstd"); c_bcast = S("c_bcast"); c_qhat = S("c_qhat")
        c_vfin = S("c_vfin"); c_vev = S("c_vev")
        c_sc = S("c_sc"); c_exp = S("c_exp"); c_msk = S("c_msk")
        c_pvt = S("c_pvt"); c_pv = S("c_pv"); c_rec = S("c_rec"); c_oev = S("c_oev")
        c_wo = S("c_wo"); c_woev = S("c_woev")
        c_gl = S("c_gl"); c_sil = S("c_sil"); c_hid = S("c_hid")
        c_h128 = S("c_h128"); c_hev = S("c_hev"); c_bf = S("c_bf")
        c_w2 = S("c_w2"); c_w2h = S("c_w2h"); c_w2e = S("c_w2e"); c_tr = S("c_tr"); c_fin = S("c_fin"); c_mev = S("c_mev")
        bV = S("bV"); bS = S("bS")

        E = st_all.enter_context
        h128_sb = E(nc.sbuf_tensor("h128_sb", [128, 128], BF16))
        ropept_sb = E(nc.sbuf_tensor("ropept_sb", [128, 2, 128], F32))
        identb_sb = E(nc.sbuf_tensor("identb_sb", [128, 128], BF16))
        identf_sb = E(nc.sbuf_tensor("identf_sb", [128, 128], F32))
        lam_sb = E(nc.sbuf_tensor("lam_sb", [128, 8], F32))
        jidx_sb = E(nc.sbuf_tensor("jidx_sb", [128, 32], U32))
        ones_col_bf = E(nc.sbuf_tensor("ones_col_bf", [128, 1], BF16))
        ones_row_f = E(nc.sbuf_tensor("ones_row_f", [1, 128], F32))
        eps_ln = E(nc.sbuf_tensor("eps_ln", [128, 1], F32))
        eps_rms = E(nc.sbuf_tensor("eps_rms", [1, 1], F32))
        qbias = E(nc.sbuf_tensor("qbias", [1, 1], F32))
        zbias = E(nc.sbuf_tensor("zbias", [1, 1], F32))
        zcol = E(nc.sbuf_tensor("zcol", [128, 1], F32))
        lamdots = E(nc.sbuf_tensor("lamdots", [1, 2], F32))
        lamexp = E(nc.sbuf_tensor("lamexp", [128, 2], F32))
        neglam = E(nc.sbuf_tensor("neglam", [128, 1], F32))
        ps = [E(nc.psum_tensor(f"ps{i}", [128, 512], F32)) for i in range(7)]

        STOP = STOP_AFTER

        def tkc(s):
            return 4 * (s + 1)

        exp_idx, pvt_idx, grp_idx, msk_idx = {}, {}, {}, {}
        EXPCUM = {0: 0}
        n = m = g = mk = 0
        for hh in range(2):
            for s in range(8):
                for i in range(2):
                    g += 1
                    grp_idx[(hh, i, s)] = g
                    for t in range(tkc(s)):
                        n += 1
                        exp_idx[(hh, i, s, t)] = n
                        if t < tkc(s) - 1:
                            m += 1
                            pvt_idx[(hh, i, s, t)] = m
                    for t in range(4 * s, 4 * s + 4):
                        mk += 1
                        msk_idx[(hh, i, s, t)] = mk
                    EXPCUM[g] = n

        cast_tiles = []
        for (srcT, dstT, nr, ncol, scale) in (
                (wv_s, wv_scr, 512, C, None),
                (wo_in, wo_scr, C, C, 1.0 - LAM_INIT),
                (w1_in, w1_scr, H, C, None),
                (w3_in, w3_scr, H, C, None),
                (w2_in, w2_scr, C, H, None)):
            for r in range(nr // 128):
                for ch in range(ncol // 1024):
                    cast_tiles.append((srcT[128 * r:128 * (r + 1), 1024 * ch:1024 * (ch + 1)],
                                       dstT[128 * r:128 * (r + 1), 1024 * ch:1024 * (ch + 1)],
                                       scale))
        N_CAST = len(cast_tiles)
        assert N_CAST == 424
        CAST_WV, CAST_WO, CAST_W1, CAST_W3, CAST_W2 = 8, 40, 168, 296, 424
        CAST_B1, CAST_B3 = 40, 296

        def cast_dma_steps(sync, t_lo, t_hi, cwf, cwb):
            for t in range(t_lo, t_hi):
                if t == 0:
                    dwci[0].dinc(sync.dma_start(out=cwf[:, 0, :], in_=cast_tiles[0][0]))
                    dwci[1].dinc(sync.dma_start(out=cwf[:, 1, :], in_=cast_tiles[1][0]))
                sync.wait_ge(c_cast.h, t + 1)
                dwco[t % 2].dinc(sync.dma_start(out=cast_tiles[t][1], in_=cwb[:, t % 2, :]))
                if t + 2 < N_CAST:
                    dwci[t % 2].dinc(sync.dma_start(out=cwf[:, t % 2, :], in_=cast_tiles[t + 2][0]))

        def cast_ops(vector, t_lo, t_hi, cwf, cwb):
            for t in range(t_lo, t_hi):
                vector.wait_ge(dwci[t % 2].h, 16 * (t // 2 + 1))
                if t >= 2:
                    vector.wait_ge(dwco[t % 2].h, 16 * (t // 2))
                src, dst, scale = cast_tiles[t]
                if scale is None:
                    c_cast.inc(vector.tensor_copy(cwb[:, t % 2, :], cwf[:, t % 2, :]))
                else:
                    c_cast.inc(vector.tensor_scalar_mul(cwb[:, t % 2, :], cwf[:, t % 2, :], scale))

        def wco_wait(eng, n_tiles):
            eng.wait_ge(dwco[0].h, 16 * ((n_tiles + 1) // 2))
            eng.wait_ge(dwco[1].h, 16 * (n_tiles // 2))

        strips = []
        for fc in range(16):
            strips.append((wv_scr[0:512, 128 * fc:128 * (fc + 1)],
                           wvt_scr[128 * fc:128 * (fc + 1), 0:512], 512))
        for fc in range(16):
            for hf in range(2):
                strips.append((w1_scr[4096 * hf:4096 * (hf + 1), 128 * fc:128 * (fc + 1)],
                               w1t_scr[128 * fc:128 * (fc + 1), 4096 * hf:4096 * (hf + 1)], 4096))
        for fc in range(16):
            for hf in range(2):
                strips.append((w3_scr[4096 * hf:4096 * (hf + 1), 128 * fc:128 * (fc + 1)],
                               w3t_scr[128 * fc:128 * (fc + 1), 4096 * hf:4096 * (hf + 1)], 4096))
        for hc in range(64):
            strips.append((w2_scr[0:C, 128 * hc:128 * (hc + 1)],
                           w2t_scr[128 * hc:128 * (hc + 1), 0:C], C))
        N_STRIP = len(strips)
        XN_S, WV_S, W1_S, W3_S, W2_S = 0, 16, 48, 80, 144
        assert N_STRIP == 144

        def strip_steps(sync, i_lo, i_hi, tb):
            for i in range(i_lo, i_hi):
                src, dst, width = strips[i]
                if i >= 2:
                    sync.wait_ge(dtw[i % 2].h, 16 * (i // 2))
                dtr[i % 2].dinc(sync.dma_start_transpose(tb[:, i % 2, 0:width], src))
                sync.wait_ge(dtr[i % 2].h, 16 * (i // 2 + 1))
                dtw[i % 2].dinc(sync.dma_start(out=dst, in_=tb[:, i % 2, 0:width]))

        def xn_strip_steps(sync, sl, tbuf):
            def slot(fc):
                return tbuf[:, fc % 2, 512 * (fc // 2):512 * (fc // 2 + 1)]
            if sl >= 1:
                sync.wait_ge(dtwx.h, 256 * sl)
            for fc in range(16):
                dtrx.dinc(sync.dma_start_transpose(
                    slot(fc), xn_scr[512 * sl:512 * (sl + 1), 128 * fc:128 * (fc + 1)]))
            sync.wait_ge(dtrx.h, 256 * (sl + 1))
            for fc in range(16):
                dtwx.dinc(sync.dma_start(
                    out=xnt_scr[128 * fc:128 * (fc + 1), 512 * sl:512 * (sl + 1)],
                    in_=slot(fc)))

        def tw_wait(eng, n_strips):
            eng.wait_ge(dtw[0].h, 16 * ((n_strips + 1) // 2))
            eng.wait_ge(dtw[1].h, 16 * (n_strips // 2))

        with nc.Block() as blk:
            @blk.sync
            def _(sync):
                d_c.dinc(sync.dma_start(out=h128_sb[:, :], in_=h128_in[:, :]))
                d_c.dinc(sync.dma_start(out=ropept_sb[:, :, :], in_=ropept_in[:, :, :]))
                d_c.dinc(sync.dma_start(out=identb_sb[:, :], in_=identb_in[:, :]))
                d_c.dinc(sync.dma_start(out=identf_sb[:, :], in_=identf_in[:, :]))
                d_c.dinc(sync.dma_start(out=jidx_sb[:, :], in_=jidx_in[:, :]))
                for t, ap in enumerate((lq1_in, lk1_in, lq2_in, lk2_in)):
                    d_c.dinc(sync.dma_start(out=lam_sb[:, 2 * t:2 * t + 2], in_=ap[:, :]))

            @blk.vector
            def _(vector):
                vector.memset(ones_col_bf[:, :], 1.0)
                vector.memset(ones_row_f[0:1, :], 1.0)
                vector.memset(eps_ln[:, :], LN_EPS)
                vector.memset(eps_rms[0:1, :], RMS_EPS)
                vector.memset(zbias[0:1, :], 0.0)
                vector.memset(zcol[:, :], 0.0)
                c_lam.inc(vector.memset(qbias[0:1, :], -0.5 * float(np.log(128.0))))

            @blk.tensor
            def _(tensor):
                tensor.wait_ge(d_c.h, d_c.v)
                tensor.wait_ge(c_lam.h, 1)
                tensor.matmul(ps[0][0:1, 0:1], lam_sb[:, 0:1], lam_sb[:, 2:3], start=True, stop=False)
                tensor.matmul(ps[0][0:1, 0:1], lam_sb[:, 1:2], lam_sb[:, 3:4], start=False, stop=True)
                tensor.matmul(ps[1][0:1, 0:1], lam_sb[:, 4:5], lam_sb[:, 6:7], start=True, stop=False)
                c_lam.inc(tensor.matmul(ps[1][0:1, 0:1], lam_sb[:, 5:6], lam_sb[:, 7:8], start=False, stop=True))

            @blk.scalar
            def _(scalar):
                scalar.wait_ge(c_lam.h, 2)
                scalar.copy(lamdots[0:1, 0:1], ps[0][0:1, 0:1])
                c_lam.inc(scalar.copy(lamdots[0:1, 1:2], ps[1][0:1, 0:1]))

        with nc.Block() as blk:
            @blk.tensor
            def _(tensor):
                tensor.wait_ge(c_lam.h, 3)
                c_lam.inc(tensor.matmul(ps[2][:, 0:2], ones_row_f[0:1, :], lamdots[0:1, 0:2],
                                        start=True, stop=True))

            @blk.scalar
            def _(scalar):
                scalar.wait_ge(c_lam.h, 4)
                c_lam.inc(scalar.activation(out=lamexp[:, :], in_=ps[2][:, 0:2], func=AF.Exp))

            @blk.vector
            def _(vector):
                vector.wait_ge(c_lam.h, 5)
                c_lam.inc(vector.tensor_scalar(
                    out=neglam[:, 0:1], in0=lamexp[:, 1:2],
                    scalar1=lamexp[:, 0:1], scalar2=-LAM_INIT,
                    op0=ALU.subtract, op1=ALU.add))
        LAM_DONE = 6

        NT_FULL, NT_MINE = 32, 8
        NT_LN = NT_FULL + NT_MINE
        ln_srcs = [x_full[128 * t:128 * (t + 1), :] for t in range(NT_FULL)] + \
                  [x_mine[128 * t:128 * (t + 1), :] for t in range(NT_MINE)]
        ln_dsts = [xn_scr[128 * t:128 * (t + 1), :] for t in range(NT_FULL)] + \
                  [xnm_scr[128 * t:128 * (t + 1), :] for t in range(NT_MINE)]

        with ExitStack() as stCast:
            EC = stCast.enter_context
            cwf = EC(nc.sbuf_tensor("cwf", [128, 2, 1024], F32))
            cwb = EC(nc.sbuf_tensor("cwb", [128, 2, 1024], BF16))

            with ExitStack() as stMid:
                EM = stMid.enter_context
                qkwT = EM(nc.sbuf_tensor("qkwT", [128, 8, 16, 128], BF16))
                qk_sb = EM(nc.sbuf_tensor("qk_sb", [128, 8, 4096], BF16))
                v_sb = EM(nc.sbuf_tensor("v_sb", [128, 2, 32, 257], BF16))
                tbuf = EM(nc.sbuf_tensor("tbuf", [128, 2, 4096], BF16, is_transpose_dest=True))

                with ExitStack() as stLN:
                    EL = stLN.enter_context
                    xbuf = EL(nc.sbuf_tensor("xbuf", [128, 3, 2048], F32))
                    xnbuf = EL(nc.sbuf_tensor("xnbuf", [128, 2, 2048], BF16))
                    mraw = EL(nc.sbuf_tensor("mraw", [128, 2], F32))
                    negm = EL(nc.sbuf_tensor("negm", [128, 2], F32))
                    sx2 = EL(nc.sbuf_tensor("sx2", [128, 2], F32))
                    m2 = EL(nc.sbuf_tensor("m2", [128, 2], F32))
                    varbuf = EL(nc.sbuf_tensor("varbuf", [128, 2], F32))
                    rstdln = EL(nc.sbuf_tensor("rstdln", [128, 2], F32))
                    qkf = EL(nc.sbuf_tensor("qkf", [128, 2, 2048], F32))

                    qk_tiles = []
                    for hl in range(2):
                        for half in range(2):
                            qk_tiles.append((wq_s[256 * hl + 128 * half:256 * hl + 128 * half + 128, :], hl))
                        for half in range(2):
                            qk_tiles.append((wk_s[256 * hl + 128 * half:256 * hl + 128 * half + 128, :], hl))

                    with nc.Block() as blk:
                        @blk.tensor
                        def _(tensor):
                            if STOP < 1:
                                return
                            tensor.wait_ge(c_lam.h, LAM_DONE)
                            for t in range(8):
                                hl = qk_tiles[t][1]
                                tensor.wait_ge(dqk[t % 2].h, 16 * (t // 2 + 1))
                                for fc in range(16):
                                    k = 16 * t + fc
                                    if k >= 4:
                                        tensor.wait_ge(c_fold.h, k - 3)
                                    c_fmm.inc(tensor.matmul(
                                        ps[k % 4][:, 0:128],
                                        qkf[:, t % 2, 128 * fc:128 * (fc + 1)],
                                        ropept_sb[:, hl, :], start=True, stop=True))

                        @blk.scalar
                        def _(scalar):
                            if STOP < 1:
                                return
                            for t in range(3):
                                dx[t % 3].dinc(scalar.dma_start(out=xbuf[:, t, :], in_=ln_srcs[t]))
                            for t in range(2):
                                dqk[t % 2].dinc(scalar.dma_start(out=qkf[:, t % 2, :],
                                                                 in_=qk_tiles[t][0]))
                            for t in range(NT_LN):
                                w = t % 2
                                scalar.wait_ge(dx[t % 3].h, 16 * (t // 3 + 1))
                                if t >= 2:
                                    scalar.wait_ge(dxn[w].h, 16 * (t // 2))
                                    scalar.wait_ge(c_var.h, t - 1)
                                c_sx2.inc(scalar.activation(
                                    out=xnbuf[:, w, :], in_=xbuf[:, t % 3, :],
                                    func=AF.Square, bias=zcol[:, 0:1],
                                    scale=1.0, accum_out=sx2[:, w:w + 1]))
                                if t >= 1:
                                    scalar.wait_ge(c_lnx.h, t)
                                    dxn[(t - 1) % 2].dinc(scalar.dma_start(
                                        out=ln_dsts[t - 1], in_=xnbuf[:, (t - 1) % 2, :]))
                                    if t + 2 < NT_LN:
                                        dx[(t + 2) % 3].dinc(scalar.dma_start(
                                            out=xbuf[:, (t + 2) % 3, :], in_=ln_srcs[t + 2]))
                                scalar.wait_ge(c_var.h, t + 1)
                                bS.inc(scalar.activation(out=rstdln[:, w:w + 1],
                                                         in_=varbuf[:, w:w + 1],
                                                         func=AF.Ln, bias=eps_ln[:, 0:1],
                                                         scale=1.0))
                                scalar.wait_ge(bS.h, bS.v)
                                c_lns.inc(scalar.activation(out=rstdln[:, w:w + 1],
                                                            in_=rstdln[:, w:w + 1],
                                                            func=AF.Exp, scale=-0.5))
                                if 8 <= t < 40:
                                    k0 = 4 * (t - 8)
                                    scalar.wait_ge(c_fmm.h, k0 + 4)
                                    for k in range(k0, k0 + 4):
                                        c_fold.inc(scalar.copy(qkwT[:, k // 16, k % 16, :],
                                                               ps[k % 4][:, 0:128]))
                                        if k % 16 == 15 and k // 16 + 2 < 8:
                                            j = k // 16 + 2
                                            dqk[j % 2].dinc(scalar.dma_start(
                                                out=qkf[:, j % 2, :], in_=qk_tiles[j][0]))
                            scalar.wait_ge(c_lnx.h, NT_LN)
                            dxn[(NT_LN - 1) % 2].dinc(scalar.dma_start(
                                out=ln_dsts[NT_LN - 1], in_=xnbuf[:, (NT_LN - 1) % 2, :]))

                        @blk.vector
                        def _(vector):
                            if STOP < 1:
                                return

                            def emit_xn(t):
                                w = t % 2
                                vector.wait_ge(c_lns.h, t + 1)
                                if t >= 2:
                                    vector.wait_ge(dxn[w].h, 16 * (t // 2))
                                c_lnx.inc(vector.tensor_scalar(
                                    out=xnbuf[:, w, :], in0=xbuf[:, t % 3, :],
                                    scalar1=negm[:, w:w + 1], scalar2=rstdln[:, w:w + 1],
                                    op0=ALU.add, op1=ALU.mult))
                                cast_ops(vector, t, t + 1, cwf, cwb)

                            for t in range(NT_LN):
                                w = t % 2
                                vector.wait_ge(dx[t % 3].h, 16 * (t // 3 + 1))
                                bV.inc(vector.reduce_sum(mraw[:, w:w + 1], xbuf[:, t % 3, :],
                                                         axis=AXL.X))
                                vector.wait_ge(bV.h, bV.v)
                                bV.inc(vector.tensor_scalar_mul(negm[:, w:w + 1],
                                                                mraw[:, w:w + 1], -1.0 / 2048.0))
                                bV.inc(vector.scalar_tensor_tensor(
                                    out=m2[:, w:w + 1], in0=mraw[:, w:w + 1],
                                    scalar=1.0 / (2048.0 * 2048.0), in1=mraw[:, w:w + 1],
                                    op0=ALU.mult, op1=ALU.mult))
                                vector.wait_ge(bV.h, bV.v)
                                vector.wait_ge(c_sx2.h, t + 1)
                                if t >= 2:
                                    vector.wait_ge(c_lns.h, t - 1)
                                c_var.inc(vector.tensor_scalar(
                                    out=varbuf[:, w:w + 1], in0=sx2[:, w:w + 1],
                                    scalar1=1.0 / 2048.0, scalar2=m2[:, w:w + 1],
                                    op0=ALU.mult, op1=ALU.subtract))
                                if t >= 1:
                                    emit_xn(t - 1)
                            emit_xn(NT_LN - 1)

                        @blk.sync
                        def _(sync):
                            if STOP < 1:
                                return
                            for sl in range(8):
                                sync.wait_ge(dxn[0].h, 16 * (2 * sl + 2))
                                sync.wait_ge(dxn[1].h, 16 * (2 * sl + 2))
                                xn_strip_steps(sync, sl, tbuf)

                        @blk.gpsimd
                        def _(gpsimd):
                            if STOP < 1:
                                return
                            cast_dma_steps(gpsimd, 0, CAST_B1, cwf, cwb)


                with ExitStack() as stB3:
                    E3 = stB3.enter_context
                    xnT = E3(nc.sbuf_tensor("xnT", [128, 16, 512], BF16))
                    wvT = E3(nc.sbuf_tensor("wvT", [128, 16, 512], BF16))
                    q2bf = E3(nc.sbuf_tensor("q2bf", [128, 4, 512], BF16))
                    qraw = E3(nc.sbuf_tensor("qraw", [128, 4, 512], F32))
                    rstdq = E3(nc.sbuf_tensor("rstdq", [1, 1024], F32))

                    with nc.Block() as blk:
                        @blk.gpsimd
                        def _(gpsimd):
                            if STOP < 2:
                                return
                            cast_dma_steps(gpsimd, CAST_B1, CAST_B3, cwf, cwb)

                        @blk.sync
                        def _(sync):
                            if STOP < 2:
                                return
                            sync.wait_ge(dxn[0].h, 16 * (NT_LN // 2))
                            sync.wait_ge(dxn[1].h, 16 * (NT_LN // 2))
                            sync.wait_ge(c_fold.h, 128)
                            wco_wait(sync, CAST_WV)
                            sync.wait_ge(dtwx.h, 16 * 128)
                            strip_steps(sync, 0, WV_S, tbuf)
                            tw_wait(sync, WV_S)
                            d_wvt.dinc(sync.dma_start(
                                out=wvT[:, :, :],
                                in_=bass.AP(wvt_scr, 0, [[512, 128], [128 * 512, 16], [1, 512]])))
                            for s in range(8):
                                if s >= 1:
                                    sync.wait_ge(c_vfin.h, 4 * s)
                                sync.wait_ge(dtwx.h, 256 * (s + 1))
                                d_xnt.dinc(sync.dma_start(
                                    out=xnT[:, :, :],
                                    in_=bass.AP(xnt_scr, 512 * s,
                                                [[T, 128], [128 * T, 16], [1, 512]])))

                        @blk.tensor
                        def _(tensor):
                            if STOP < 2:
                                return
                            tensor.wait_ge(c_fold.h, 128)
                            tensor.wait_ge(d_wvt.h, 16)
                            for s in range(8):
                                tensor.wait_ge(d_xnt.h, 16 * (s + 1))
                                for hm in range(8):
                                    idx = 8 * s + hm
                                    bank = ps[hm % 4]
                                    if idx >= 4:
                                        tensor.wait_ge(c_sq.h, idx - 3)
                                    if s >= 1:
                                        tensor.wait_ge(c_vev.h, 8 * s)
                                    last = None
                                    for fc in range(16):
                                        last = tensor.matmul(bank[:, 0:512], qkwT[:, hm, fc, :],
                                                             xnT[:, fc, :],
                                                             start=(fc == 0), stop=(fc == 15))
                                    c_pmm.inc(last)
                                    if hm in (3, 7):
                                        W = 2 * s + (hm // 4)
                                        jb = idx - 3
                                        for half in range(2):
                                            j0 = jb + 2 * half
                                            for j in (j0, j0 + 1):
                                                tensor.wait_ge(c_sq.h, j + 1)
                                                if j % 2 == 0 and j >= 2:
                                                    tensor.wait_ge(c_mev.h, j - 1)
                                                if j % 2 == 1 and j >= 3:
                                                    tensor.wait_ge(c_qhat.h, j - 2)
                                                c_msq.inc(tensor.matmul(
                                                    ps[4 + j % 2][0:1, 0:512], ones_col_bf[:, 0:1],
                                                    q2bf[:, j % 4, :], start=True, stop=True))
                                            tensor.wait_ge(c_rstd.h, 2 * W + half + 1)
                                            for j in (j0, j0 + 1):
                                                if j % 2 == 0:
                                                    tensor.wait_ge(c_mev.h, j + 2)
                                                elif j >= 2:
                                                    tensor.wait_ge(c_qhat.h, j - 1)
                                                c_bcast.inc(tensor.matmul(
                                                    ps[5 + j % 2][:, 0:512], ones_row_f[0:1, :],
                                                    rstdq[0:1, 512 * (j % 2):512 * (j % 2 + 1)],
                                                    start=True, stop=True))
                                tensor.wait_ge(c_qhat.h, 8 * (s + 1))
                                for tsub in range(4):
                                    bank = ps[tsub]
                                    if s >= 1:
                                        tensor.wait_ge(c_vev.h, 8 * (s - 1) + 2 * (tsub + 1))
                                    last = None
                                    for fc in range(16):
                                        last = tensor.matmul(
                                            bank[:, 0:512],
                                            xnT[:, fc, 128 * tsub:128 * (tsub + 1)],
                                            wvT[:, fc, :], start=(fc == 0), stop=(fc == 15))
                                    c_vfin.inc(last)

                        @blk.scalar
                        def _(scalar):
                            if STOP < 2:
                                return
                            for s in range(8):
                                for hm in range(8):
                                    idx = 8 * s + hm
                                    scalar.wait_ge(c_pmm.h, idx + 1)
                                    if idx >= 4:
                                        scalar.wait_ge(c_msq.h, idx - 3)
                                        scalar.wait_ge(c_qhat.h, idx - 3)
                                    scalar.activation(out=q2bf[:, idx % 4, :],
                                                      in_=ps[hm % 4][:, 0:512], func=AF.Square)
                                    c_sq.inc(scalar.copy(qraw[:, idx % 4, :], ps[hm % 4][:, 0:512]))
                                    if hm in (3, 7):
                                        W = 2 * s + (hm // 4)
                                        jb = idx - 3
                                        for half in range(2):
                                            j0 = jb + 2 * half
                                            for j in (j0, j0 + 1):
                                                scalar.wait_ge(c_msq.h, j + 1)
                                                if j >= 2:
                                                    scalar.wait_ge(c_bcast.h, j - 1)
                                                c_mev.inc(scalar.copy(
                                                    rstdq[0:1, 512 * (j % 2):512 * (j % 2 + 1)],
                                                    ps[4 + j % 2][0:1, 0:512]))
                                            scalar.wait_ge(c_mev.h, j0 + 2)
                                            bS.inc(scalar.activation(
                                                out=rstdq[0:1, :], in_=rstdq[0:1, :],
                                                func=AF.Ln, bias=eps_rms[0:1, 0:1],
                                                scale=1.0 / 128.0))
                                            scalar.wait_ge(bS.h, bS.v)
                                            c_rstd.inc(scalar.activation(
                                                out=rstdq[0:1, :], in_=rstdq[0:1, :],
                                                func=AF.Exp, scale=-0.5,
                                                bias=(qbias[0:1, 0:1] if half == 0
                                                      else zbias[0:1, 0:1])))

                        @blk.vector
                        def _(vector):
                            if STOP < 2:
                                return
                            vector.wait_ge(dxn[0].h, 16 * (NT_LN // 2))
                            vector.wait_ge(dxn[1].h, 16 * (NT_LN // 2))
                            vector.memset(v_sb[:, :, :, 256:257], 1.0)
                            for s in range(8):
                                for hm in range(8):
                                    idx = 8 * s + hm
                                    vector.wait_ge(c_bcast.h, idx + 1)
                                    c_qhat.inc(vector.tensor_mul(
                                        qk_sb[:, hm, 512 * s:512 * (s + 1)],
                                        qraw[:, idx % 4, :], ps[5 + idx % 2][:, 0:512]))
                                for tsub in range(4):
                                    vector.wait_ge(c_vfin.h, 4 * s + tsub + 1)
                                    bank = ps[tsub]
                                    c_vev.inc(vector.tensor_copy(v_sb[:, 0, 4 * s + tsub, 0:256],
                                                                 bank[:, 0:256]))
                                    c_vev.inc(vector.tensor_copy(v_sb[:, 1, 4 * s + tsub, 0:256],
                                                                 bank[:, 256:512]))
                                cast_ops(vector, CAST_B1 + 32 * s, CAST_B1 + 32 * (s + 1), cwf, cwb)

                with ExitStack() as stB4:
                    E4 = stB4.enter_context
                    masks_sb = E4(nc.sbuf_tensor("masks_sb", [128, 4, 512], BF16))
                    p_ring = E4(nc.sbuf_tensor("p_ring", [128, 3, 512], BF16))
                    tmp1 = E4(nc.sbuf_tensor("tmp1", [128, 1024], F32))
                    ybf = E4(nc.sbuf_tensor("ybf", [128, 2, 4, 256], BF16))
                    rc = E4(nc.sbuf_tensor("rc", [128, 4], F32))
                    nl2 = E4(nc.sbuf_tensor("nl2", [128, 4], F32))
                    tbuf4 = E4(nc.sbuf_tensor("tbuf4", [128, 2, 4096], BF16, is_transpose_dest=True))

                    with nc.Block() as blk:
                        @blk.tensor
                        def _(tensor):
                            if STOP < 3:
                                return
                            tensor.wait_ge(c_qhat.h, 64)
                            tensor.wait_ge(c_vev.h, 64)

                            def emit_pv(tensor, hh, i, s, tt, g, Ts):
                                boundary = tt >= 4 * s
                                last = None
                                for q4 in range(4):
                                    if q4 == 0:
                                        tensor.wait_ge(c_exp.h, exp_idx[(hh, i, s, tt)])
                                        if boundary:
                                            tensor.wait_ge(c_msk.h, msk_idx[(hh, i, s, tt)])
                                        if tt == 0:
                                            tensor.wait_ge(c_oev.h, 4 * (g - 1))
                                    last = tensor.matmul(
                                        ps[q4][:, 0:257],
                                        p_ring[:, tt % 3, 128 * q4:128 * (q4 + 1)],
                                        v_sb[:, hh, tt, :],
                                        start=(tt == 0), stop=(tt == Ts - 1))
                                if tt == Ts - 1:
                                    c_pv.inc(last)
                                else:
                                    c_pvt.inc(last)

                            for hh in range(2):
                                for s in range(8):
                                    for i in range(2):
                                        qhm = 4 * hh + i
                                        khm = 4 * hh + 2 + i
                                        g = grp_idx[(hh, i, s)]
                                        Ts = tkc(s)
                                        for tt in range(Ts):
                                            if tt >= 2:
                                                tensor.wait_ge(c_exp.h, exp_idx[(hh, i, s, tt - 2)])
                                            elif g >= 2:
                                                tensor.wait_ge(c_exp.h, EXPCUM[g - 1])
                                            c_sc.inc(tensor.matmul(
                                                ps[4 + tt % 2][:, 0:512],
                                                qk_sb[:, khm, 128 * tt:128 * (tt + 1)],
                                                qk_sb[:, qhm, 512 * s:512 * (s + 1)],
                                                start=True, stop=True))
                                            if tt >= 1:
                                                emit_pv(tensor, hh, i, s, tt - 1, g, Ts)
                                        emit_pv(tensor, hh, i, s, Ts - 1, g, Ts)

                        @blk.scalar
                        def _(scalar):
                            if STOP < 3:
                                return
                            for hh in range(2):
                                for s in range(8):
                                    for i in range(2):
                                        g = grp_idx[(hh, i, s)]
                                        Ts = tkc(s)
                                        for tt in range(Ts):
                                            scalar.wait_ge(c_sc.h, exp_idx[(hh, i, s, tt)])
                                            if tt >= 3:
                                                scalar.wait_ge(c_pvt.h, pvt_idx[(hh, i, s, tt - 3)])
                                            elif g >= 2:
                                                scalar.wait_ge(c_pv.h, g - 1)
                                            c_exp.inc(scalar.activation(
                                                out=p_ring[:, tt % 3, :],
                                                in_=ps[4 + tt % 2][:, 0:512], func=AF.Exp))

                        @blk.vector
                        def _(vector):
                            if STOP < 3:
                                return
                            vector.wait_ge(c_lam.h, LAM_DONE)
                            vector.wait_ge(d_msk.h, 16)
                            for hh in range(2):
                                for s in range(8):
                                    for i in range(2):
                                        g = grp_idx[(hh, i, s)]
                                        for tt in range(4 * s, 4 * s + 4):
                                            vector.wait_ge(c_exp.h, exp_idx[(hh, i, s, tt)])
                                            c_msk.inc(vector.tensor_mul(
                                                p_ring[:, tt % 3, :], p_ring[:, tt % 3, :],
                                                masks_sb[:, tt - 4 * s, :]))
                                        vector.wait_ge(c_pv.h, g)
                                        if i == 0:
                                            for q4 in range(4):
                                                inst = vector.reciprocal(rc[:, q4:q4 + 1],
                                                                         ps[q4][:, 256:257])
                                            c_rec.inc(inst)
                                            vector.wait_ge(c_rec.h, c_rec.v)
                                            for q4 in range(4):
                                                c_oev.inc(vector.tensor_scalar_mul(
                                                    tmp1[:, 256 * q4:256 * (q4 + 1)],
                                                    ps[q4][:, 0:256], rc[:, q4:q4 + 1]))
                                        else:
                                            for q4 in range(4):
                                                inst = vector.reciprocal(rc[:, q4:q4 + 1],
                                                                         ps[q4][:, 256:257])
                                            c_rec.inc(inst)
                                            vector.wait_ge(c_rec.h, c_rec.v)
                                            for q4 in range(4):
                                                inst = vector.tensor_mul(nl2[:, q4:q4 + 1],
                                                                         rc[:, q4:q4 + 1],
                                                                         neglam[:, 0:1])
                                            c_rec.inc(inst)
                                            vector.wait_ge(c_rec.h, c_rec.v)
                                            vector.wait_ge(c_oev.h, 4 * grp_idx[(hh, 0, s)])
                                            widx = 8 * hh + s + 1
                                            if widx >= 3:
                                                vector.wait_ge(dy[widx % 2].h,
                                                               64 * ((widx - 1) // 2))
                                            for q4 in range(4):
                                                c_oev.inc(vector.scalar_tensor_tensor(
                                                    out=ybf[:, s % 2, q4, :],
                                                    in0=ps[q4][:, 0:256],
                                                    scalar=nl2[:, q4:q4 + 1],
                                                    in1=tmp1[:, 256 * q4:256 * (q4 + 1)],
                                                    op0=ALU.mult, op1=ALU.add))
                                            gi = 8 * hh + s
                                            cast_ops(vector, CAST_B3 + 8 * gi,
                                                     CAST_B3 + 8 * (gi + 1), cwf, cwb)


                        @blk.sync
                        def _(sync):
                            if STOP < 3:
                                return
                            tw_wait(sync, WV_S)
                            sync.wait_ge(c_vfin.h, 32)
                            sync.wait_ge(c_qhat.h, 64)
                            sync.wait_ge(c_msq.h, 64)
                            d_msk.dinc(sync.dma_start(out=masks_sb[:, :, :], in_=masks_in[:, :, :]))
                            for hh in range(2):
                                for s in range(8):
                                    widx = 8 * hh + s + 1
                                    sync.wait_ge(c_oev.h, 4 * grp_idx[(hh, 1, s)])
                                    for q4 in range(4):
                                        dy[widx % 2].dinc(sync.dma_start(
                                            out=ycc_in[512 * s + 128 * q4:512 * s + 128 * (q4 + 1),
                                                       256 * hh:256 * (hh + 1)],
                                            in_=ybf[:, s % 2, q4, :]))
                                    gi = 8 * hh + s
                                    if gi == 0:
                                        wco_wait(sync, CAST_W1)
                                    if gi == 8:
                                        wco_wait(sync, CAST_W3)
                                    strip_steps(sync, WV_S + 4 * gi, WV_S + 4 * (gi + 1), tbuf4)

                        @blk.gpsimd
                        def _(gpsimd):
                            if STOP < 3:
                                return
                            cast_dma_steps(gpsimd, CAST_B3, N_CAST, cwf, cwb)
                            gpsimd.wait_ge(dy[0].h, 64 * 8)
                            gpsimd.wait_ge(dy[1].h, 64 * 8)
                            c_cc.inc(gpsimd.collective_compute(
                                "AllGather", ALU.bypass,
                                replica_groups=[[0, 1, 2, 3], [4, 5, 6, 7]],
                                ins=[ycc_in.ap().opt()], outs=[ycc_out.ap().opt()]))

            with ExitStack() as stB5:
                E5 = stB5.enter_context
                tbuf5 = E5(nc.sbuf_tensor("tbuf5", [128, 2, 4096], BF16, is_transpose_dest=True))
                ysl2 = E5(nc.sbuf_tensor("ysl2", [128, 2, 8, 512], BF16))
                yT = E5(nc.sbuf_tensor("yT", [128, 16, 1024], BF16, is_transpose_dest=True))
                woT = E5(nc.sbuf_tensor("woT", [128, 16, 2048], BF16, is_transpose_dest=True))
                attstg = E5(nc.sbuf_tensor("attstg", [128, 2, 512], F32))

                with nc.Block() as blk:
                    @blk.sync
                    def _(sync):
                        if STOP < 4:
                            return
                        wco_wait(sync, CAST_WO)
                        for yc in range(16):
                            d_wot.dinc(sync.dma_start_transpose(
                                woT[:, yc, :], wo_scr[0:2048, 128 * yc:128 * (yc + 1)]))
                        wco_wait(sync, CAST_W2)
                        strip_steps(sync, W3_S, W3_S + 32, tbuf5)
                        sync.wait_ge(dysw[0].h, 32)
                        sync.wait_ge(dysw[1].h, 32)
                        for r in range(4):
                            for sub in range(4):
                                d_yt2.dinc(sync.dma_start_transpose(
                                    yT[:, 4 * r + sub, :],
                                    ysl_scr[TS * r:TS * (r + 1), 128 * sub:128 * (sub + 1)]))
                        strip_steps(sync, W3_S + 32, W2_S, tbuf5)

                    @blk.gpsimd
                    def _(gpsimd):
                        if STOP < 4:
                            return
                        gpsimd.wait_ge(c_cc.h, 1)
                        gpsimd.wait_ge(d_c.h, d_c.v)
                        for r in range(4):
                            if r >= 2:
                                gpsimd.wait_ge(dysw[r % 2].h, 16 * (r // 2))
                            for tt in range(8):
                                dycc[r % 2].dinc(gpsimd.indirect_dma_start(
                                    out=ysl2[:, r % 2, tt, :], out_offset=None,
                                    in_=ycc_out[:, :],
                                    in_offset=bass.IndirectOffsetOnAxis(
                                        ap=jidx_sb[:, 8 * r + tt:8 * r + tt + 1], axis=0)))
                            gpsimd.wait_ge(dycc[r % 2].h, 128 * (r // 2 + 1))
                            ap = bass.AP(ysl_scr, r * TS * 512,
                                         [[512, 128], [128 * 512, 8], [1, 512]])
                            dysw[r % 2].dinc(gpsimd.dma_start(out=ap, in_=ysl2[:, r % 2, :, :]))

                    @blk.tensor
                    def _(tensor):
                        if STOP < 4:
                            return
                        tensor.wait_ge(d_yt2.h, 16 * 16)
                        tensor.wait_ge(d_wot.h, 16 * 16)
                        for dc in range(16):
                            for th in range(2):
                                idx = 2 * dc + th
                                bank = ps[4 + idx % 2]
                                if idx >= 2:
                                    tensor.wait_ge(c_woev.h, idx - 1)
                                last = None
                                for yc in range(16):
                                    last = tensor.matmul(
                                        bank[:, 0:512], woT[:, yc, 128 * dc:128 * (dc + 1)],
                                        yT[:, yc, 512 * th:512 * (th + 1)],
                                        start=(yc == 0), stop=(yc == 15))
                                c_wo.inc(last)

                    @blk.vector
                    def _(vector):
                        if STOP < 4:
                            return
                        for idx in range(32):
                            vector.wait_ge(c_wo.h, idx + 1)
                            if idx >= 2:
                                vector.wait_ge(datt[idx % 2].h, 16 * (idx // 2))
                            c_woev.inc(vector.tensor_copy(attstg[:, idx % 2, :],
                                                          ps[4 + idx % 2][:, 0:512]))

                    @blk.scalar
                    def _(scalar):
                        if STOP < 4:
                            return
                        scalar.wait_ge(c_cc.h, 1)
                        for dc in range(16):
                            for th in range(2):
                                idx = 2 * dc + th
                                scalar.wait_ge(c_woev.h, idx + 1)
                                datt[idx % 2].dinc(scalar.dma_start(
                                    out=attn_scr[128 * dc:128 * (dc + 1), 512 * th:512 * (th + 1)],
                                    in_=attstg[:, idx % 2, :]))

        with ExitStack() as st6:
            E6 = st6.enter_context
            xnTs = E6(nc.sbuf_tensor("xnTs", [128, 16, 1024], BF16, is_transpose_dest=True))
            hidden = E6(nc.sbuf_tensor("hidden", [128, 72, 512], BF16))
            w1w = E6(nc.sbuf_tensor("w1w", [128, 2, 16, 256], BF16))
            w3w = E6(nc.sbuf_tensor("w3w", [128, 2, 16, 256], BF16))
            w2t = E6(nc.sbuf_tensor("w2t", [128, 2, 2048], BF16))
            attb = E6(nc.sbuf_tensor("attb", [128, 2, 512], F32))
            sumfm = E6(nc.sbuf_tensor("sumfm", [128, 16, 512], F32))
            xres = E6(nc.sbuf_tensor("xres", [128, 2048], F32))
            outstg = E6(nc.sbuf_tensor("outstg", [128, 2048], F32))
            silu_s = E6(nc.sbuf_tensor("silu_s", [128, 2, 512], F32))
            ps7 = E6(nc.psum_tensor("ps7", [128, 512], F32))

            def butterfly_sched():
                sm = list(range(64))
                pool = list(range(64, 72))
                freed = []
                opsl = []
                npair = 0
                for k in range(6):
                    for a in range(64):
                        if a & (1 << k):
                            continue
                        bpair = a | (1 << k)
                        bar = (npair % 4 == 0) and npair > 0
                        if npair % 4 == 0 and npair > 0:
                            pool.extend(freed)
                            freed = []
                        s0 = pool.pop(0)
                        s1 = pool.pop(0)
                        opsl.append((s0, sm[a], sm[bpair], True, bar))
                        opsl.append((s1, sm[a], sm[bpair], False, False))
                        freed.extend([sm[a], sm[bpair]])
                        sm[a], sm[bpair] = s0, s1
                        npair += 1
                return opsl, sm

            BF_OPS, BF_SLOT = butterfly_sched()

            with nc.Block() as blk:
                @blk.sync
                def _(sync):
                    if STOP < 5:
                        return
                    sync.wait_ge(c_cast.h, N_CAST)
                    sync.wait_ge(c_wo.h, 32)
                    sync.wait_ge(datt[0].h, 256)
                    sync.wait_ge(datt[1].h, 256)
                    tw_wait(sync, W2_S)
                    for fc in range(16):
                        d_xnmt.dinc(sync.dma_start_transpose(
                            xnTs[:, fc, :], xnm_scr[0:TS, 128 * fc:128 * (fc + 1)]))
                    for p in range(2):
                        for hw in range(32):
                            gw = 32 * p + hw
                            if gw >= 2:
                                sync.wait_ge(c_gl.h, 4 * (gw - 1))
                            for (wt, buf) in ((w1t_scr, w1w), (w3t_scr, w3w)):
                                dw13[gw % 2].dinc(sync.dma_start(
                                    out=buf[:, gw % 2, :, :],
                                    in_=bass.AP(wt, 256 * hw,
                                                [[H, 128], [128 * H, 16], [1, 256]])))
                        for dcg in range(2):
                            bblk = 2 * p + dcg
                            for hc in range(64):
                                widx = bblk * 64 + hc
                                if widx >= 2:
                                    if hc == 1:
                                        sync.wait_ge(c_w2.h, 8 * bblk)
                                    elif hc == 0:
                                        sync.wait_ge(c_w2h.h, 63 * bblk)
                                    else:
                                        sync.wait_ge(c_w2h.h, 63 * bblk + hc - 1)
                                dw2t[widx % 2].dinc(sync.dma_start(
                                    out=w2t[:, hc % 2, :],
                                    in_=w2t_scr[128 * hc:128 * (hc + 1), 0:2048]))

                @blk.scalar
                def _(scalar):
                    if STOP < 5:
                        return
                    scalar.wait_ge(datt[0].h, 256)
                    scalar.wait_ge(datt[1].h, 256)
                    for p in range(2):
                        for ch in range(64):
                            gg = 64 * p + ch
                            scalar.wait_ge(c_gl.h, 2 * gg + 1)
                            if gg >= 2:
                                scalar.wait_ge(c_hid.h, gg - 1)
                            c_sil.inc(scalar.activation(out=silu_s[:, ch % 2, :],
                                                        in_=ps[ch % 2][:, 0:512], func=AF.Silu))
                        for dcg in range(2):
                            for dcl in range(8):
                                dc = 8 * dcg + dcl
                                aidx = (2 * p + dcg) * 8 + dcl
                                if aidx >= 2:
                                    scalar.wait_ge(c_w2e.h, aidx - 1)
                                datr[aidx % 2].dinc(scalar.dma_start(
                                    out=attb[:, dcl % 2, :],
                                    in_=attn_scr[128 * dc:128 * (dc + 1), 512 * p:512 * (p + 1)]))
                        for sub in range(4):
                            bidx = 4 * p + sub
                            if bidx >= 1:
                                scalar.wait_ge(c_fin.h, 16 * bidx)
                            d_xr.dinc(scalar.dma_start(
                                out=xres[:, :],
                                in_=x_mine[512 * p + 128 * sub:512 * p + 128 * (sub + 1), :]))
                            scalar.wait_ge(c_fin.h, 16 * (bidx + 1))
                            d_out.dinc(scalar.dma_start(
                                out=out[512 * p + 128 * sub:512 * p + 128 * (sub + 1), :],
                                in_=outstg[:, :]))

                @blk.tensor
                def _(tensor):
                    if STOP < 5:
                        return
                    tensor.wait_ge(c_woev.h, 32)
                    tensor.wait_ge(d_xnmt.h, 16 * 16)
                    for p in range(2):
                        if p == 1:
                            tensor.wait_ge(c_w2e.h, 16)
                        for hw in range(32):
                            gw = 32 * p + hw
                            tensor.wait_ge(dw13[gw % 2].h, 32 * (gw // 2 + 1))
                            for hcl in range(2):
                                ch = 2 * hw + hcl
                                gg = 64 * p + ch
                                gbank = ps[ch % 2]
                                lbank = ps[2 + ch % 2]
                                if gg >= 2:
                                    tensor.wait_ge(c_sil.h, gg - 1)
                                last = None
                                for fc in range(16):
                                    last = tensor.matmul(
                                        gbank[:, 0:512],
                                        w1w[:, gw % 2, fc, 128 * hcl:128 * (hcl + 1)],
                                        xnTs[:, fc, 512 * p:512 * (p + 1)],
                                        start=(fc == 0), stop=(fc == 15))
                                c_gl.inc(last)
                                if gg >= 2:
                                    tensor.wait_ge(c_hid.h, gg - 1)
                                last = None
                                for fc in range(16):
                                    last = tensor.matmul(
                                        lbank[:, 0:512],
                                        w3w[:, gw % 2, fc, 128 * hcl:128 * (hcl + 1)],
                                        xnTs[:, fc, 512 * p:512 * (p + 1)],
                                        start=(fc == 0), stop=(fc == 15))
                                c_gl.inc(last)
                        if p == 1:
                            tensor.wait_ge(c_fin.h, 64)
                        for ch in range(64):
                            tensor.wait_ge(c_hid.h, 64 * p + ch + 1)
                            if ch >= 2:
                                tensor.wait_ge(c_hev.h, 64 * p + ch - 1)
                            c_h128.inc(tensor.matmul(ps[4 + ch % 2][:, 0:512], h128_sb[:, :],
                                                     hidden[:, ch, :], start=True, stop=True))
                        tensor.wait_ge(c_bf.h, p + 1)
                        for dcg in range(2):
                            bblk = 2 * p + dcg
                            for hc in range(64):
                                widx = bblk * 64 + hc
                                tensor.wait_ge(dw2t[widx % 2].h, 16 * (widx // 2 + 1))
                                if hc == 0:
                                    tensor.wait_ge(c_w2e.h, 8 * bblk)
                                last = None
                                for dcl in range(8):
                                    bank = ps[dcl] if dcl < 7 else ps7
                                    last = tensor.matmul(
                                        bank[:, 0:512],
                                        w2t[:, hc % 2, 128 * (8 * dcg + dcl):128 * (8 * dcg + dcl + 1)],
                                        hidden[:, BF_SLOT[hc], :],
                                        start=(hc == 0), stop=(hc == 63))
                                    if hc == 63:
                                        c_w2.inc(last)
                                if hc < 63:
                                    c_w2h.inc(last)
                        tensor.wait_ge(c_w2e.h, 16 * (p + 1))
                        for sub in range(4):
                            for dc in range(16):
                                cnt = (4 * p + sub) * 16 + dc + 1
                                if cnt >= 3:
                                    tensor.wait_ge(c_fin.h, cnt - 2)
                                c_tr.inc(tensor.transpose(
                                    ps[4 + cnt % 2][:, 0:128],
                                    sumfm[:, dc, 128 * sub:128 * (sub + 1)],
                                    identf_sb[:, :]))

                @blk.vector
                def _(vector):
                    if STOP < 5:
                        return
                    for p in range(2):
                        for ch in range(64):
                            gg = 64 * p + ch
                            vector.wait_ge(c_gl.h, 2 * gg + 2)
                            vector.wait_ge(c_sil.h, gg + 1)
                            c_hid.inc(vector.tensor_mul(hidden[:, ch, :],
                                                        ps[2 + ch % 2][:, 0:512],
                                                        silu_s[:, ch % 2, :]))
                        for ch in range(64):
                            vector.wait_ge(c_h128.h, 64 * p + ch + 1)
                            c_hev.inc(vector.tensor_scalar_mul(hidden[:, ch, :],
                                                               ps[4 + ch % 2][:, 0:512], FW_SCALE))
                        vector.wait_ge(c_hev.h, 64 * (p + 1))
                        inst = None
                        for (dst, a, bsl, is_add, stage_first) in BF_OPS:
                            if stage_first and inst is not None:
                                bV.inc(inst)
                                vector.wait_ge(bV.h, bV.v)
                            if is_add:
                                inst = vector.tensor_add(hidden[:, dst, :], hidden[:, a, :],
                                                         hidden[:, bsl, :])
                            else:
                                inst = vector.tensor_sub(hidden[:, dst, :], hidden[:, a, :],
                                                         hidden[:, bsl, :])
                        c_bf.inc(inst)
                        for dcg in range(2):
                            for dcl in range(8):
                                dc = 8 * dcg + dcl
                                aidx = (2 * p + dcg) * 8 + dcl
                                vector.wait_ge(c_w2.h, 8 * (2 * p + dcg) + dcl + 1)
                                vector.wait_ge(datr[aidx % 2].h, 16 * (aidx // 2 + 1))
                                bank = ps[dcl] if dcl < 7 else ps7
                                c_w2e.inc(vector.tensor_add(sumfm[:, dc, :], bank[:, 0:512],
                                                            attb[:, dcl % 2, :]))
                        for sub in range(4):
                            bidx = 4 * p + sub
                            vector.wait_ge(d_xr.h, 16 * (bidx + 1))
                            if bidx >= 1:
                                vector.wait_ge(d_out.h, 16 * bidx)
                            for dc in range(16):
                                cnt = (4 * p + sub) * 16 + dc + 1
                                vector.wait_ge(c_tr.h, cnt)
                                c_fin.inc(vector.tensor_add(outstg[:, 128 * dc:128 * (dc + 1)],
                                                            ps[4 + cnt % 2][:, 0:128],
                                                            xres[:, 128 * dc:128 * (dc + 1)]))

                @blk.gpsimd
                def _(gpsimd):
                    if STOP < 5:
                        return
                    gpsimd.wait_ge(d_out.h, 16 * 8)

    return nc


_NC_CACHE = None


def _get_nc():
    global _NC_CACHE
    if _NC_CACHE is None:
        _NC_CACHE = build_program()
    return _NC_CACHE


def kernel(L0_attn_x, L0_attn_wq, L0_attn_wk, L0_attn_wv, L0_attn_wo,
           L0_attn_lq1, L0_attn_lk1, L0_attn_lq2, L0_attn_lk2,
           L0_ffn_W1, L0_ffn_W2, L0_ffn_W3):
    x = np.ascontiguousarray(np.asarray(L0_attn_x, dtype=np.float32))
    wq = np.ascontiguousarray(np.asarray(L0_attn_wq, dtype=np.float32))
    wk = np.ascontiguousarray(np.asarray(L0_attn_wk, dtype=np.float32))
    wv = np.ascontiguousarray(np.asarray(L0_attn_wv, dtype=np.float32))
    wo = np.ascontiguousarray(np.asarray(L0_attn_wo, dtype=np.float32))
    w1 = np.ascontiguousarray(np.asarray(L0_ffn_W1, dtype=np.float32))
    w2 = np.ascontiguousarray(np.asarray(L0_ffn_W2, dtype=np.float32))
    w3 = np.ascontiguousarray(np.asarray(L0_ffn_W3, dtype=np.float32))
    lqk = [np.ascontiguousarray(np.asarray(v, dtype=np.float32).reshape(2, 128).T)
           for v in (L0_attn_lq1, L0_attn_lk1, L0_attn_lq2, L0_attn_lk2)]
    masks, h128, ropePT, identb, identf = host_consts()

    def jidx_host(j):
        jj = np.zeros((128, 32), np.uint32)
        for r in range(4):
            for tt in range(8):
                jj[:, 8 * r + tt] = r * T + TS * j + 128 * tt + np.arange(128)
        return jj

    nc = _get_nc()
    in_maps = []
    for c in range(8):
        b, j = divmod(c, 4)
        in_maps.append({
            "x_full": x[b],
            "x_mine": np.ascontiguousarray(x[b, TS * j:TS * (j + 1)]),
            "wq_s": np.ascontiguousarray(wq[512 * j:512 * (j + 1)]),
            "wk_s": np.ascontiguousarray(wk[512 * j:512 * (j + 1)]),
            "wv_s": np.ascontiguousarray(wv[512 * j:512 * (j + 1)]),
            "wo_in": wo, "w1_in": w1, "w3_in": w3, "w2_in": w2,
            "lq1_in": lqk[0], "lk1_in": lqk[1], "lq2_in": lqk[2], "lk2_in": lqk[3],
            "masks_in": masks, "h128_in": h128,
            "ropept_in": np.ascontiguousarray(ropePT[2 * j:2 * j + 2].transpose(1, 0, 2)),
            "identb_in": identb, "identf_in": identf,
            "jidx_in": jidx_host(j),
        })
    res = run_bass_kernel_spmd(nc, in_maps, core_ids=list(range(8)))
    outp = np.empty((B, T, C), dtype=np.float32)
    for c in range(8):
        b, j = divmod(c, 4)
        outp[b, TS * j:TS * (j + 1), :] = res.results[c]["out"]
    return outp
